# version 21
# baseline (speedup 1.0000x reference)
import numpy as np
import concourse.bass as bass
import concourse.mybir as mybir

F32 = mybir.dt.float32
BF16 = mybir.dt.bfloat16
AF = mybir.ActivationFunctionType
ALU = mybir.AluOpType
AX = mybir.AxisListType.X

T = 2048
D = 1024
HF = 4096
NE = 8
NT = T // 128
DC = D // 128
W = 1024
EPS = 1e-5
SCALE_S = float(1.0 / np.sqrt(8192.0))

ENGS = ("pe", "act", "dve", "pool", "sp")
DBG = False


class Plan:
    def __init__(self, nc):
        self.nc = nc
        self.ops = {e: [] for e in ENGS}
        self.sems = {}
        self.cnt = {}

    def sem(self, name):
        if name not in self.sems:
            self.sems[name] = self.nc.alloc_semaphore(name)
            self.cnt[name] = 0
        return self.sems[name]

    def do(self, eng, fn):
        self.ops[eng].append(fn)

    def inc(self, eng, fn, sname, amt):
        s = self.sem(sname)
        self.cnt[sname] += amt

        def g(e, fn=fn, s=s, amt=amt):
            fn(e).then_inc(s, amt)

        self.ops[eng].append(g)
        return self.cnt[sname]

    def wait(self, eng, sname, val):
        if val <= 0:
            return
        s = self.sem(sname)
        self.ops[eng].append(lambda e, s=s, v=val: e.wait_ge(s, v))

    def c(self, sname):
        return self.cnt.get(sname, 0)

    def marker(self, eng):
        return self.inc(eng, lambda e: e.drain(), eng, 1)

    def dr(self, eng="dve"):
        self.ops[eng].append(lambda e: e.drain())


class SlotMgr:
    def __init__(self, plan, n, sem_prefix):
        self.plan = plan
        self.n = n
        self.names = [f"{sem_prefix}{i}" for i in range(n)]
        self.uses = [0] * n
        self.i = 0

    def acquire(self, writer_eng="pe"):
        s = self.i % self.n
        self.i += 1
        if self.uses[s] >= 1:
            self.plan.wait(writer_eng, self.names[s], self.uses[s])
        self.uses[s] += 1
        return s, self.names[s]


def build_program(taps=None, phase_limit=99, groups=None):
    taps = taps or {}
    groups = groups or [[0, 1, 2, 3], [4, 5, 6, 7]]
    nc = bass.Bass("TRN2", target_bir_lowering=False)
    p = Plan(nc)

    dx = nc.dram_tensor("x", [T, D], F32, kind="ExternalInput").ap()
    d_nw = nc.dram_tensor("norm_w", [D], F32, kind="ExternalInput").ap()
    d_nb = nc.dram_tensor("norm_b", [D], F32, kind="ExternalInput").ap()
    d_qkvw = nc.dram_tensor("qkv_w", [3 * D, D], F32, kind="ExternalInput").ap()
    d_qkvb = nc.dram_tensor("qkv_b", [3 * D], F32, kind="ExternalInput").ap()
    d_projw = nc.dram_tensor("proj_w", [D, D], F32, kind="ExternalInput").ap()
    d_projb = nc.dram_tensor("proj_b", [D], F32, kind="ExternalInput").ap()
    d_gw = nc.dram_tensor("gate_w", [D, NE], F32, kind="ExternalInput").ap()
    d_gb = nc.dram_tensor("gate_b", [NE], F32, kind="ExternalInput").ap()
    d_w1 = nc.dram_tensor("w1", [NE, D, HF], F32, kind="ExternalInput").ap()
    d_b1 = nc.dram_tensor("b1", [NE, HF], F32, kind="ExternalInput").ap()
    d_w2 = nc.dram_tensor("w2", [NE, HF, D], F32, kind="ExternalInput").ap()
    d_b2 = nc.dram_tensor("b2", [NE, D], F32, kind="ExternalInput").ap()
    d_out = nc.dram_tensor("out", [T, D], F32, kind="ExternalOutput").ap()

    wq_bf = nc.dram_tensor("wq_bf", [3 * D, D], BF16)
    wp_bf = nc.dram_tensor("wp_bf", [D, D], BF16)
    xn_dram = nc.dram_tensor("xn_dram", [D, T], F32)
    g_dram = nc.dram_tensor("g_dram", [D, T], F32)
    v_dram = nc.dram_tensor("v_dram", [D, T], BF16)
    h_dram = nc.dram_tensor("h_dram", [D, T], BF16)
    S_dram = nc.dram_tensor("S_dram", [D, D], F32)
    Ssum_dram = nc.dram_tensor("Ssum_dram", [D, D], F32)
    watt_dram = nc.dram_tensor("watt_dram", [D, D], BF16)

    tap_aps = {}
    for tname, tshape in taps.items():
        tap_aps[tname] = nc.dram_tensor("tap_" + tname, list(tshape), F32,
                                        kind="ExternalOutput").ap()

    ps_q = nc.alloc_psum_tensor("ps_q", [128, 4, 512], F32)
    ps_t32 = nc.alloc_psum_tensor("ps_t32", [128, 8, 128], F32)
    ps_tb = nc.alloc_psum_tensor("ps_tb", [128, 8, 128], BF16)
    ps_x = nc.alloc_psum_tensor("ps_x", [128, 512], F32)

    quad = SlotMgr(p, 4, "q")
    t32m = SlotMgr(p, 1, "t32_")
    tbm = SlotMgr(p, 1, "tb_")
    pxm = SlotMgr(p, 1, "px_")

    SB0 = 16512

    def at(name, shape, dt, off):
        return nc.alloc_sbuf_tensor_at(name, shape, dt, offset=off)

    po = [SB0]

    def pal(name, shape, dt):
        nb = int(np.prod(shape[1:])) * (2 if dt == BF16 else 4)
        off = (po[0] + 31) // 32 * 32
        po[0] = off + nb
        assert po[0] <= SB0 + 30000
        return at(name, shape, dt, off)

    ident_bf = pal("ident_bf", [128, 128], BF16)
    ident_f = pal("ident_f", [128, 128], F32)
    nw_rep = pal("nw_rep", [128, D], F32)
    nb_rep = pal("nb_rep", [128, D], F32)
    qkvb_v = pal("qkvb_v", [128, DC], F32)
    projb_pp = pal("projb_pp", [128, DC], F32)
    b1_pp = pal("b1_pp", [128, NE, HF // 128], F32)
    b2_lhs = pal("b2_lhs", [NE, D], F32)
    gate_w_sb = pal("gate_w_sb", [128, DC, NE], F32)
    gate_b_pp = pal("gate_b_pp", [NE, 4], F32)
    iota8 = pal("iota8", [128, NE], F32)
    ones_col = pal("ones_col", [1, 128], F32)
    stats = pal("stats", [128, 4, 16], F32)
    rowbuf = pal("rowbuf", [1, 1024], F32)
    poolscr = pal("poolscr", [1, 128], F32)
    eps_col = pal("eps_col", [128, 1], F32)
    prb0 = pal("prb0", [128, 1], F32)
    stage8 = pal("stage8", [NE, 128], F32)
    rep4 = pal("rep4", [128, 4, 512], F32)
    ones_bf = pal("ones_bf", [1, 128], BF16)

    sstage2 = at("sstage2", [128, 2, 1024], F32, 17280)
    watt_row = at("watt_row", [128, 2, 1024], BF16, 36928)

    comb_bf = at("comb_bf", [NE, T], BF16, 46528)
    b2_bf = at("b2_bf", [NE, D], BF16, 50624)
    rowbf = at("rowbf", [1, 1024], BF16, 52672)
    h_fm = at("h_fm", [128, DC, T], BF16, 54720)
    glog = at("glog", [NE, T], F32, 87488)
    x_sb = at("x_sb", [128, 3, D], F32, 95680)
    xn_tok2 = at("xn_tok2", [128, 2, D], F32, 107968)
    ztmp2 = at("ztmp2", [128, 2, D], F32, 116160)
    h_tok2 = at("h_tok2", [128, 2, D], BF16, 124352)
    xn_stage = at("xn_stage", [128, 2, DC, 512], F32, 128448)
    p15b = at("p15b", [128, 8, NT, NE], F32, 95680)
    p15s = at("p15s", [128, 8, NT], F32, 99808)
    p15dbg = at("p15dbg", [128, NT, 24], F32, 100352)
    sh = at("sh", [128, 32, W], BF16, 54720)
    y_acc = at("y_acc", [128, DC, W], F32, 120256)
    R_e = at("R_e", [128, W], F32, 153024)
    ytmp = at("ytmp", [128, 2, 512], F32, 157120)
    w1b = at("w1b", [128, 2, DC, 512], BF16, 163264)
    w2bA = at("w2bA", [128, 32, 256], BF16, 179648)
    w2bB = at("w2bB", [128, 32, 256], BF16, 196032)
    xn_half = at("xn_half", [128, DC, W], BF16, 212416)
    w2bufs = [w2bA, w2bB]
    wT = at("wT", [128, DC, 1024], BF16, 163264)
    wT2 = at("wT2", [128, DC, 1024], BF16, 144832)
    S_stage = at("S_stage", [128, 1024], F32, 71104)
    vstage = at("vstage", [128, 2, 512], BF16, 75200)
    rep_js = at("rep_js", [128, 512], F32, 77248)
    h2 = at("h2", [128, DC, T], BF16, 79296)
    pfA = at("pfA", [128, 2, DC, 512], BF16, 79296)
    pfC = at("pfC", [128, 2, DC, 512], BF16, 95680)
    pfB = at("pfB", [128, 2, DC, 512], BF16, 144832)
    q_tok = at("q_tok", [128, NT, 1024], BF16, 112064)
    k_tok = at("k_tok", [128, NT, 1024], BF16, 196032)
    P4B = 46528
    v2 = at("v2", [128, DC, T], BF16, P4B)
    sgbuf = at("sgbuf", [128, 2, 512], F32, 79296)
    a_fm = at("a_fm", [128, DC, T], BF16, 83392)
    xn_t = at("xn_t", [128, 2, 512], F32, 153024)
    g_t = at("g_t", [128, 2, 512], F32, 157120)
    ptmp = at("ptmp", [128, 512], F32, 161216)
    w_attT = at("w_attT", [128, DC, 1024], BF16, 163264)
    pwT = at("pwT", [128, DC, 1024], BF16, 179648)
    x_stage = at("x_stage", [128, 4, 1024], F32, 196032)
    o_stage = at("o_stage", [128, 4, 1024], F32, 212416)

    p.inc("pool", lambda e: e.iota(iota8[:, :], pattern=[[1, NE]], base=0,
                                   channel_multiplier=0,
                                   allow_small_or_imprecise_dtypes=True), "pool", 1)
    p.inc("pool", lambda e: e.memset(ident_bf[:, :], 0.0), "pool", 1)
    p.wait("pool", "pool", p.c("pool"))
    p.inc("pool", lambda e: e.affine_select(out=ident_bf[:, :], in_=ident_bf[:, :],
                                            compare_op=ALU.not_equal, fill=1.0, base=0,
                                            pattern=[[-1, 128]], channel_multiplier=1), "pool", 1)
    p.inc("pool", lambda e: e.memset(ident_f[:, :], 0.0), "pool", 1)
    p.wait("pool", "pool", p.c("pool"))
    p.inc("pool", lambda e: e.affine_select(out=ident_f[:, :], in_=ident_f[:, :],
                                            compare_op=ALU.not_equal, fill=1.0, base=0,
                                            pattern=[[-1, 128]], channel_multiplier=1), "pool", 1)
    p.wait("pool", "pool", p.c("pool"))
    c_pool_const = p.inc("pool", lambda e: e.tensor_copy(poolscr[0:1, :], ident_f[0:1, :]),
                         "pool", 1)

    p.inc("sp", lambda e: e.dma_start(out=b2_lhs[:, :], in_=d_b2), "sw0", 16)
    p.inc("sp", lambda e: e.dma_start(out=gate_w_sb[:, :, :],
                                      in_=d_gw.rearrange("(c p) e -> p c e", p=128)), "sw0", 16)
    p.inc("sp", lambda e: e.dma_start(out=gate_b_pp[:, 0:1], in_=d_gb[:, None]), "sw0", 16)
    c_sp_small = p.c("sw0")

    c_ones = p.inc("dve", lambda e: e.memset(ones_col[:, :], 1.0), "dve", 1)
    p.inc("dve", lambda e: e.memset(ones_bf[:, :], 1.0), "dve", 1)
    p.inc("dve", lambda e: e.memset(eps_col[:, :], EPS), "dve", 1)
    p.inc("dve", lambda e: e.memset(prb0[:, :], 0.0), "dve", 1)

    def rep_row(dst, src_row_load):
        p.wait("sp", "pe", p.c("pe"))
        c_row = p.inc("sp", src_row_load, "srow", 16)
        for ns in range(2):
            s_, sname = pxm.acquire()
            p.wait("pe", "srow", c_row)
            p.wait("pe", "dve", c_ones)
            cpe = p.inc("pe", lambda e, ns=ns: e.matmul(
                ps_x[:, :], ones_col[:, :], rowbuf[:, ns * 512:(ns + 1) * 512],
                start=True, stop=True), "pe", 1)
            p.wait("dve", "pe", cpe)
            p.inc("dve", lambda e, ns=ns, dst=dst: e.tensor_copy(
                dst[:, ns * 512:(ns + 1) * 512], ps_x[:, :]), sname, 1)
            p.dr()

    rep_row(nw_rep, lambda e: e.dma_start(out=rowbuf[0:1, :], in_=d_nw[None, :]))
    rep_row(nb_rep, lambda e: e.dma_start(out=rowbuf[0:1, :], in_=d_nb[None, :]))

    def build_rep4():
        for idx in range(4):
            base_j = (idx // 2) * 1024 + (idx % 2) * 512
            p.dr()
            p.wait("sp", "pe", p.c("pe"))
            c_row = p.inc("sp", lambda e, base_j=base_j: e.dma_start(
                out=rowbuf[0:1, 0:512], in_=d_qkvb[None, base_j:base_j + 512]), "srw2", 16)
            sx_, nx = pxm.acquire()
            p.wait("pe", "srw2", c_row)
            p.wait("pe", "dve", p.c("dve"))
            cpe = p.inc("pe", lambda e: e.matmul(ps_x[:, :], ones_col[:, :], rowbuf[0:1, 0:512],
                                                 start=True, stop=True), "pe", 1)
            p.wait("dve", "pe", cpe)
            p.inc("dve", lambda e, idx=idx: e.tensor_copy(rep4[:, idx, :], ps_x[:, :]), nx, 1)
            p.dr()
        return p.marker("dve")
    c_rep4 = build_rep4()


    b1_8p = at("b1_8p", [NE, HF], F32, 163264)
    c_b1 = p.inc("sp", lambda e: e.dma_start(out=b1_8p[:, :], in_=d_b1), "sb1", 16)
    p.wait("pe", "sb1", c_b1)
    p.wait("pe", "pool", c_pool_const)
    for hc in range(HF // 128):
        s_, sname = t32m.acquire()
        cpe = p.inc("pe", lambda e, hc=hc: e.transpose(
            ps_t32[:, 0, 0:NE], b1_8p[:, hc * 128:(hc + 1) * 128], ident_f[0:NE, 0:NE]),
            "pe", 1)
        p.wait("dve", "pe", cpe)
        p.inc("dve", lambda e, hc=hc: e.tensor_copy(b1_pp[:, :, hc], ps_t32[:, 0, 0:NE]),
              sname, 1)
    c_b1_rd = p.c("pe")
    for (src_ap, dst) in ((d_qkvb[2 * D:3 * D], qkvb_v), (d_projb, projb_pp)):
        p.wait("sp", "pe", p.c("pe"))
        c_ld = p.inc("sp", lambda e, src_ap=src_ap: e.dma_start(
            out=stage8[:, :], in_=src_ap.rearrange("(c p) -> c p", p=128)), "sst8", 16)
        s_, n32c = t32m.acquire()
        p.wait("pe", "sst8", c_ld)
        c_tr = p.inc("pe", lambda e: e.transpose(ps_t32[:, 0, 0:NE], stage8[:, :],
                                                 ident_f[0:NE, 0:NE]), "pe", 1)
        p.wait("dve", "pe", c_tr)
        p.inc("dve", lambda e, dst=dst: e.tensor_copy(dst[:, :], ps_t32[:, 0, 0:NE]), n32c, 1)
    p.wait("dve", "sw0", c_sp_small)
    p.inc("dve", lambda e: e.tensor_copy(b2_bf[:, :], b2_lhs[:, :]), "dve", 1)
    p.dr()
    c_b1_done = p.marker("dve")

    x_loads = {}
    for i0 in range(3):
        x_loads[i0] = p.inc("sp", lambda e, i0=i0: e.dma_start(
            out=x_sb[:, i0, :], in_=dx[i0 * 128:(i0 + 1) * 128, :]), "sx%d" % i0, 16)

    p.wait("pool", "dve", c_b1_done)
    sqrt_done = {}
    ca_stage_of = {}
    ch_of = {}
    x_done = {}
    xn_group_done = {}
    tr32_done = {}
    trh_done = {}
    gate_done = {}
    glog_done = []
    for tc in range(NT):
        xv = x_sb[:, tc % 3, :]
        xn_tok = xn_tok2[:, tc % 2, :]
        ztmp = ztmp2[:, tc % 2, :]
        h_tok = h_tok2[:, tc % 2, :]
        p.wait("dve", "sx%d" % (tc % 3), 16 * (tc // 3 + 1))

        def ln_statsA(src, pidx):
            st_ = stats[:, pidx % 4, :]
            if pidx >= 4:
                p.wait("dve", "act", sqrt_done[pidx - 4])
            p.inc("dve", lambda e, src=src, st_=st_: e.bn_stats(st_[:, 0:6], src[:, 0:512]), "dve", 1)
            p.inc("dve", lambda e, src=src, st_=st_: e.bn_stats(st_[:, 6:12], src[:, 512:1024]), "dve", 1)
            p.dr()
            p.inc("dve", lambda e, st_=st_: e.bn_aggr(st_[:, 12:14], st_[:, 0:12]), "dve", 1)
            p.dr()
            cd = p.c("dve")
            p.wait("act", "dve", cd)
            ca_sq = p.inc("act", lambda e, st_=st_: e.activation(st_[:, 14:15], st_[:, 13:14],
                                                                 AF.Sqrt, bias=eps_col[:, 0:1]), "act", 1)
            sqrt_done[pidx] = ca_sq

        def ln_finishB(src, dst, pidx):
            st_ = stats[:, pidx % 4, :]
            p.wait("dve", "act", sqrt_done[pidx])
            p.inc("dve", lambda e, st_=st_: e.reciprocal(st_[:, 15:16], st_[:, 14:15]), "dve", 1)
            p.inc("dve", lambda e, st_=st_: e.tensor_scalar(st_[:, 13:14], st_[:, 12:13], -1.0, None, ALU.mult), "dve", 1)
            p.dr()
            cdst = p.inc("dve", lambda e, src=src, dst=dst, st_=st_: e.tensor_scalar(
                dst, src, st_[:, 13:14], st_[:, 15:16], ALU.add, ALU.mult), "dve", 1)
            p.dr()
            return cdst

        if tc >= 2:
            p.wait("dve", "pe", tr32_done[tc - 2])
            p.wait("dve", "pool", ch_of[tc - 2])
            p.wait("pool", "pe", trh_done[tc - 2])
        if tc == 0:
            ln_statsA(xv, 0)
        cd_xn = ln_finishB(xv, xn_tok, 2 * tc)
        x_done[tc] = cd_xn
        if tc + 3 < NT:
            p.wait("sp", "dve", cd_xn)
            x_loads[tc + 3] = p.inc("sp", lambda e, tc=tc: e.dma_start(
                out=x_sb[:, tc % 3, :], in_=dx[(tc + 3) * 128:(tc + 4) * 128, :]),
                "sx%d" % (tc % 3), 16)

        ln_statsA(xn_tok, 2 * tc + 1)
        if tc + 1 < NT:
            xv_n = x_sb[:, (tc + 1) % 3, :]
            p.wait("dve", "sx%d" % ((tc + 1) % 3), 16 * ((tc + 1) // 3 + 1))
            ln_statsA(xv_n, 2 * (tc + 1))
        cd_z = ln_finishB(xn_tok, ztmp, 2 * tc + 1)
        p.wait("pool", "dve", cd_z)
        p.inc("pool", lambda e, ztmp=ztmp: e.tensor_tensor(ztmp, ztmp, nw_rep[:, :], ALU.mult), "pool", 1)
        p.dr("pool")
        ch = p.inc("pool", lambda e, ztmp=ztmp, h_tok=h_tok: e.tensor_tensor(h_tok, ztmp, nb_rep[:, :], ALU.add), "pool", 1)
        p.dr("pool")

        s32, n32 = t32m.acquire()
        p.wait("pe", "dve", cd_xn)
        for j in range(8):
            fn = lambda e, j=j, xn_tok=xn_tok: e.transpose(ps_t32[:, j, :], xn_tok[:, j * 128:(j + 1) * 128], ident_f[:, :])
            if j < 7:
                p.do("pe", fn)
            else:
                c_tr = p.inc("pe", fn, "pe", 1)
        tr32_done[tc] = c_tr
        if tc >= 1:
            emit_gate(tc - 1)
            emit_htr(tc - 1)
        if tc % 4 == 0 and tc >= 8:
            g_, cnt_ = xn_group_done[tc // 4 - 2]
            p.wait("act", g_, cnt_)
            p.wait("act", "pe", gate_done[(tc // 4 - 2) * 4 + 3])
        p.wait("act", "pe", c_tr)
        ca_stage = p.inc("act", lambda e, tc=tc: e.activation(
            xn_stage[:, (tc // 4) % 2, :, (tc % 4) * 128:(tc % 4) * 128 + 128],
            ps_t32[:, :, :], AF.Copy), "act", 1)
        ca_stage_of[tc] = ca_stage
        p.inc("act", lambda e: e.drain(), n32, 1)

        def emit_gate(tg):
            p.wait("pe", "act", ca_stage_of[tg])
            if tg == 0:
                p.wait("pe", "sw0", c_sp_small)
            sx_, nx = pxm.acquire()
            for dc in range(DC):
                fn = lambda e, dc=dc, tg=tg: e.matmul(
                    ps_x[0:NE, 0:128], gate_w_sb[:, dc, :],
                    xn_stage[:, (tg // 4) % 2, dc, (tg % 4) * 128:(tg % 4) * 128 + 128],
                    start=(dc == 0), stop=(dc == DC - 1))
                if dc < DC - 1:
                    p.do("pe", fn)
                else:
                    c_g = p.inc("pe", fn, "pe", 1)
            gate_done[tg] = c_g
            p.wait("dve", "pe", c_g)
            glog_done.append(p.inc("dve", lambda e, tg=tg: e.tensor_scalar(
                glog[:, tg * 128:(tg + 1) * 128], ps_x[0:NE, 0:128], gate_b_pp[:, 0:1],
                None, ALU.add), "dve", 1))
            p.inc("dve", lambda e: e.drain(), nx, 1)

        def emit_htr(tg):
            h_tg = h_tok2[:, tg % 2, :]
            sb_, nb_ = tbm.acquire()
            p.wait("pe", "pool", ch_of[tg])
            for j in range(8):
                fn = lambda e, j=j, h_tg=h_tg: e.transpose(ps_tb[:, j, :], h_tg[:, j * 128:(j + 1) * 128], ident_bf[:, :])
                if j < 7:
                    p.do("pe", fn)
                else:
                    c_trh = p.inc("pe", fn, "pe", 1)
            trh_done[tg] = c_trh
            p.wait("act", "pe", c_trh)
            p.inc("act", lambda e, tg=tg: e.activation(
                h_fm[:, :, tg * 128:(tg + 1) * 128], ps_tb[:, :, :], AF.Copy), "act", 1)
            p.inc("act", lambda e: e.drain(), nb_, 1)

        ch_of[tc] = ch
        if tc % 4 == 3:
            grp = tc // 4
            p.wait("sp", "act", ca_stage)
            xn_group_done[grp] = ("sxn%d" % (grp % 2), p.inc("sp", lambda e, grp=grp: e.dma_start(
                out=xn_dram.ap().rearrange("(c p) t -> p c t", p=128)[:, :, grp * 512:(grp + 1) * 512],
                in_=xn_stage[:, grp % 2, :, :]), "sxn%d" % (grp % 2), 16))

    emit_gate(NT - 1)
    emit_htr(NT - 1)
    c_p1_act = p.c("act")
    p.wait("sp", "act", c_p1_act)
    c_hsp = p.inc("sp", lambda e: e.dma_start(
        out=h_dram.ap().rearrange("(c p) t -> p c t", p=128), in_=h_fm[:, :, :]), "shsp", 16)

    if "xn_fm" in tap_aps:
        for gi_ in range(4):
            p.wait("sp", xn_group_done[gi_][0], xn_group_done[gi_][1])
        p.inc("sp", lambda e: e.dma_start(out=tap_aps["xn_fm"], in_=xn_dram.ap()), "stap", 16)
    if "glog" in tap_aps:
        p.wait("sp", "dve", p.c("dve"))
        p.inc("sp", lambda e: e.dma_start(out=tap_aps["glog"], in_=glog[:, :]), "stap", 16)
    if "h_fm" in tap_aps:
        p.wait("sp", "shsp", c_hsp)
        p.inc("sp", lambda e: e.dma_start(out=tap_aps["h_fm"], in_=h_dram.ap()), "stap", 16)

    if phase_limit < 2:
        return finish(nc, p, tap_aps)

    from concourse.bass_types import AP as _AP

    def bc3(ap2d, inner=NE):
        return _AP(tensor=ap2d.tensor, offset=ap2d.offset,
                   ap=[list(ap2d.ap[0]), list(ap2d.ap[1]), [0, inner]])

    def bcmid(ap2d, mid=NT):
        return _AP(tensor=ap2d.tensor, offset=ap2d.offset,
                   ap=[list(ap2d.ap[0]), [0, mid], list(ap2d.ap[1])])

    PT = p15b[:, 0, :, :]
    PL2 = p15b[:, 1, :, :]
    OH1 = p15b[:, 2, :, :]
    OH2 = p15b[:, 3, :, :]
    CMB = p15b[:, 4, :, :]
    NMX = p15s[:, 0, :]
    SSM = p15s[:, 1, :]
    M1 = p15s[:, 2, :]
    M2 = p15s[:, 3, :]
    I1 = p15s[:, 4, :]
    I2 = p15s[:, 5, :]
    WS = p15s[:, 6, :]
    IOTA3 = bcmid(iota8[:, :])

    tr_done = 0
    for tc in range(NT):
        s32, n32 = t32m.acquire()
        p.wait("pe", "dve", glog_done[tc])
        c_tr = p.inc("pe", lambda e, tc=tc: e.transpose(
            ps_t32[:, 0, 0:NE], glog[:, tc * 128:(tc + 1) * 128], ident_f[0:NE, 0:NE]), "pe", 1)
        p.wait("dve", "pe", c_tr)
        p.inc("dve", lambda e, tc=tc: e.tensor_copy(PT[:, tc, :], ps_t32[:, 0, 0:NE]), n32, 1)
        p.dr()

    def TT(out, a, b, op):
        p.inc("dve", lambda e, out=out, a=a, b=b, op=op: e.tensor_tensor(out, a, b, op), "dve", 1)
        p.dr()

    def TS(out, a, s1, s2, op0, op1=None):
        if op1 is None:
            p.inc("dve", lambda e, out=out, a=a, s1=s1, op0=op0: e.tensor_scalar(out, a, s1, None, op0), "dve", 1)
        else:
            p.inc("dve", lambda e, out=out, a=a, s1=s1, s2=s2, op0=op0, op1=op1: e.tensor_scalar(out, a, s1, s2, op0, op1), "dve", 1)
        p.dr()

    def RED(out, a, op, negate=False):
        p.inc("dve", lambda e, out=out, a=a, op=op, negate=negate: e.tensor_reduce(
            out, a, axis=AX, op=op, negate=negate), "dve", 1)
        p.dr()

    RED(NMX, PT, ALU.max, negate=True)
    TT(PL2, PT, bc3(NMX), ALU.add)
    cdve = p.c("dve")
    p.wait("act", "dve", cdve)
    ca_exp = p.inc("act", lambda e: e.activation(
        PL2.rearrange("p a b -> p (a b)"), PL2.rearrange("p a b -> p (a b)"), AF.Exp), "act", 1)
    p.wait("dve", "act", ca_exp)
    RED(SSM, PL2, ALU.add)
    p.inc("dve", lambda e: e.reciprocal(SSM, SSM), "dve", 1)
    p.dr()
    TT(PT, PL2, bc3(SSM), ALU.mult)
    RED(M1, PT, ALU.max)
    TT(PL2, PT, bc3(M1), ALU.is_ge)
    TT(OH1, IOTA3, PL2, ALU.mult)
    TS(PL2, PL2, -1.0, -9.0, ALU.add, ALU.mult)
    TT(OH1, OH1, PL2, ALU.add)
    RED(I1, OH1, ALU.min)
    TT(OH1, IOTA3, bc3(I1), ALU.is_equal)
    TT(PL2, OH1, bc3(M1), ALU.mult)
    TS(PL2, PL2, 1e9, None, ALU.mult)
    TT(PL2, PT, PL2, ALU.subtract)
    RED(M2, PL2, ALU.max)
    TT(PL2, PL2, bc3(M2), ALU.is_ge)
    TT(OH2, IOTA3, PL2, ALU.mult)
    TS(PL2, PL2, -1.0, -9.0, ALU.add, ALU.mult)
    TT(OH2, OH2, PL2, ALU.add)
    RED(I2, OH2, ALU.min)
    TT(OH2, IOTA3, bc3(I2), ALU.is_equal)
    TT(WS, M1, M2, ALU.add)
    TS(WS, WS, 1e-9, None, ALU.max)
    p.inc("dve", lambda e: e.reciprocal(WS, WS), "dve", 1)
    p.dr()
    TT(OH1, OH1, bc3(M1), ALU.mult)
    TT(OH2, OH2, bc3(M2), ALU.mult)
    TT(CMB, OH1, OH2, ALU.add)
    TT(CMB, CMB, bc3(WS), ALU.mult)
    ccomb = p.marker("dve")

    for tc in range(NT):
        s32b, n32b = t32m.acquire()
        p.wait("pe", "dve", ccomb)
        c_trc = p.inc("pe", lambda e, tc=tc: e.transpose(
            ps_t32[0:NE, 0, :], CMB[:, tc, :], ident_f[:, :]), "pe", 1)
        p.wait("dve", "pe", c_trc)
        p.inc("dve", lambda e, tc=tc: e.tensor_copy(comb_bf[:, tc * 128:(tc + 1) * 128], ps_t32[0:NE, 0, :]), n32b, 1)
        p.inc("dve", lambda e: e.drain(), "dve", 1)
    c_comb_done = p.c("dve")

    if "comb" in tap_aps:
        p.wait("sp", "dve", c_comb_done)
        p.inc("sp", lambda e: e.dma_start(out=tap_aps["comb"], in_=comb_bf[:, :]), "stap", 16)
    if "p15dbg" in tap_aps:
        p.wait("sp", "dve", p.c("dve"))
        p.inc("sp", lambda e: e.dma_start(out=tap_aps["p15dbg"], in_=p15dbg[:, :, :]), "stap", 16)

    if phase_limit < 3:
        return finish(nc, p, tap_aps)

    st3 = dict(w1_uses=[0, 0], w2_uses=[0, 0], w1_pe_done=[0, 0], w2_pe_done=[0, 0],
               mm1_pe_last=0, mm2_gate=0, g_silu={}, g_store={},
               casts_emitted=False, c_wq_cast=0, c_wp_cast=0, xh_loads=0,
               k_readers_pe=0)

    st3["exp_done"] = {}
    st3["watt_done"] = {}
    st3["watt_store"] = {}
    st3["pse_uses"] = 0

    def wide_psum(base):
        return _AP(tensor=base.tensor, offset=base.offset,
                   ap=[list(base.ap[0]), [1, 512]])
    psE = (wide_psum(ps_t32[:, 0, :]), wide_psum(ps_t32[:, 4, :]))

    st3["sst2_free"] = {}

    def emit_softmax_qfc(qfcs):
        cd_max = {}
        for qfc in qfcs:
            slot = qfc % 2
            if qfc == 0:
                p.wait("sp", "pool", ch_of[NT - 1])
                p.wait("sp", "cc", 1)
            if qfc >= 2:
                p.wait("sp", "dve", st3["sst2_free"][qfc - 2])
            p.inc("sp", lambda e, qfc=qfc, slot=slot: e.dma_start(
                out=sstage2[:, slot, :], in_=Ssum_dram.ap()[qfc * 128:(qfc + 1) * 128, :]),
                "ssl%d" % slot, 16)
        for qfc in qfcs:
            slot = qfc % 2
            st_ = stats[:, slot, :]
            p.wait("dve", "ssl%d" % slot, 16 * (qfc // 2 + 1))
            if qfc >= 2:
                p.wait("dve", "act", st3["exp_done"][qfc - 2])
            cd_max[qfc] = p.inc("dve", lambda e, slot=slot, st_=st_: e.reduce_max(
                st_[:, 0:1], sstage2[:, slot, :], axis=AX, negate=True), "dve", 1)
            p.dr()
        for qfc in qfcs:
            slot = qfc % 2
            st_ = stats[:, slot, :]
            p.wait("act", "dve", cd_max[qfc])
            st3["exp_done"][qfc] = p.inc("act", lambda e, slot=slot, st_=st_: e.activation(
                sstage2[:, slot, :], sstage2[:, slot, :], AF.Exp, bias=st_[:, 0:1]), "act", 1)
        for qfc in qfcs:
            slot = qfc % 2
            st_ = stats[:, slot, :]
            p.wait("dve", "act", st3["exp_done"][qfc])
            p.inc("dve", lambda e, slot=slot, st_=st_: e.reduce_sum(st_[:, 1:2], sstage2[:, slot, :], axis=AX), "dve", 1)
            p.dr()
            p.inc("dve", lambda e, st_=st_: e.reciprocal(st_[:, 2:3], st_[:, 1:2]), "dve", 1)
            p.dr()
            if qfc >= 2:
                p.wait("dve", "swr%d" % slot, 16 * (qfc // 2))
            p.inc("dve", lambda e, slot=slot, st_=st_: e.tensor_scalar(
                watt_row[:, slot, :], sstage2[:, slot, :], st_[:, 2:3], None, ALU.mult), "dve", 1)
            cm = p.marker("dve")
            st3["sst2_free"][qfc] = cm
            p.wait("sp", "dve", cm)
            st3["watt_store"][qfc] = p.inc("sp", lambda e, qfc=qfc, slot=slot: e.dma_start(
                out=watt_dram.ap()[qfc * 128:(qfc + 1) * 128, :], in_=watt_row[:, slot, :]),
                "swr%d" % slot, 16)

    def emit_watt_loads():
        p.wait("sp", "pe", st3["mm1_pe_last"])
        p.wait("sp", "swr0", 16 * 4)
        p.wait("sp", "swr1", 16 * 4)
        for kfc in range(DC):
            st3["c_wat"] = p.inc("sp", lambda e, kfc=kfc: e.dma_start(
                out=w_attT[:, kfc, :], in_=watt_dram.ap()[:, kfc * 128:(kfc + 1) * 128],
                transpose=True), "swat", 16)

    def emit_p3_half(half, hook_after=None, hook_at=2, late_hook=None):
        if half == 0:
            p.wait("pool", xn_group_done[0][0], xn_group_done[0][1])
            p.wait("pool", xn_group_done[1][0], xn_group_done[1][1])
        else:
            p.wait("pool", xn_group_done[2][0], xn_group_done[2][1])
            p.wait("pool", xn_group_done[3][0], xn_group_done[3][1])
            p.wait("pool", "pe", st3["mm1_pe_last"])
            p.wait("pool", "pe", st3["k_readers_pe"])
        for ch in range(4):
            st3["xh_loads"] += 1
            p.inc("pool", lambda e, half=half, ch=ch: e.dma_start(
                out=xn_half[:, ch * 2:(ch + 1) * 2, :],
                in_=xn_dram.ap()[ch * 256:(ch + 1) * 256,
                                 half * W:(half + 1) * W].rearrange("(c p) t -> p c t", p=128)),
                "sxh", 16)

        if half == 0:
            for gi_ in range(4):
                p.wait("dve", xn_group_done[gi_][0], xn_group_done[gi_][1])
        else:
            p.wait("dve", "act", st3["g_silu"][0])
            p.wait("dve", "sg", st3["g_store"][0])

        def emit_bias_init(half=half):
            p.wait("pe", "dve", c_comb_done)
            for m in range(DC):
                for n in range(2):
                    sl, sn = quad.acquire()
                    cpe = p.inc("pe", lambda e, sl=sl, m=m, n=n, half=half: e.matmul(
                        ps_q[:, sl, :], b2_bf[:, m * 128:(m + 1) * 128],
                        comb_bf[:, half * W + n * 512:half * W + (n + 1) * 512],
                        start=True, stop=True), "pe", 1)
                    p.wait("dve", "pe", cpe)
                    p.inc("dve", lambda e, sl=sl, m=m, n=n: e.tensor_copy(
                        y_acc[:, m, n * 512:(n + 1) * 512], ps_q[:, sl, :]), sn, 1)
            p.dr()

        def emit_Re(e_, half=half):
            p.wait("pe", "dve", c_comb_done)
            p.wait("act", "dve", p.c("dve"))
            for n in range(2):
                sx_, nx_ = pxm.acquire()
                cpe = p.inc("pe", lambda e, e_=e_, n=n, half=half: e.matmul(
                    ps_x[0:1, :], ident_bf[0:NE, e_:e_ + 1],
                    comb_bf[:, half * W + n * 512:half * W + (n + 1) * 512],
                    start=True, stop=True), "pe", 1)
                p.wait("act", "pe", cpe)
                p.inc("act", lambda e, n=n: e.activation(rowbf[0:1, n * 512:(n + 1) * 512],
                                                         ps_x[0:1, :], AF.Copy), nx_, 1)
            c_row_act = p.marker("act")
            for n in range(2):
                sx_, nx_ = pxm.acquire()
                p.wait("pe", "act", c_row_act)
                cpe = p.inc("pe", lambda e, n=n: e.matmul(
                    ps_x[:, :], ones_bf[:, :], rowbf[0:1, n * 512:(n + 1) * 512],
                    start=True, stop=True), "pe", 1)
                p.wait("act", "pe", cpe)
                p.inc("act", lambda e, n=n: e.activation(R_e[:, n * 512:(n + 1) * 512],
                                                         ps_x[:, :], AF.Copy), nx_, 1)
            return p.marker("act")

        for e_ in range(NE):
            gi = half * NE + e_
            if e_ > 0:
                c_re_dve = emit_Re(e_)
            if isinstance(hook_after, dict):
                if e_ in hook_after:
                    hook_after[e_]()
            elif hook_after is not None and e_ == hook_at:
                hook_after()

            if gi == 0:
                p.inc("pool", lambda e: e.dma_start(
                    out=w2bufs[0][:, :, :],
                    in_=d_w2[0, :, 0:256].rearrange("(c p) d -> p c d", p=128)),
                    "sw20", 16)
                st3["w2_uses"][0] += 1

            for s8 in range(8):
                slot = (gi * 8 + s8) % 2
                sem_w1 = "sw1%d" % slot
                pfbuf = None
                if gi == NE and s8 >= 2:
                    if s8 < 4:
                        pfbuf = pfA[:, s8 - 2, :, :]
                        p.wait("pe", "spfa", 32)
                    elif s8 < 6:
                        pfbuf = pfC[:, s8 - 4, :, :]
                        p.wait("pe", "spfc", 32)
                    else:
                        pfbuf = pfB[:, s8 - 6, :, :]
                        p.wait("pe", "spfb", 32)
                else:
                    if st3["w1_uses"][slot] >= 1:
                        p.wait("pool", "pe", st3["w1_pe_done"][slot])
                    if gi == 0 and s8 < 2:
                        p.wait("pool", "pe", c_b1_rd)
                    if gi == NE and s8 < 2:
                        p.wait("pool", "pe", st3["v_pe_last"])
                    p.inc("pool", lambda e, e_=e_, s8=s8, slot=slot: e.dma_start(
                        out=w1b[:, slot, :, :],
                        in_=d_w1[e_, :, s8 * 512:(s8 + 1) * 512].rearrange("(c p) h -> p c h", p=128)),
                        sem_w1, 16)
                    st3["w1_uses"][slot] += 1
                    p.wait("pe", sem_w1, 16 * st3["w1_uses"][slot])
                if s8 == 0:
                    p.wait("pe", "sxh", 16 * st3["xh_loads"])
                    if gi == 0:
                        p.wait("act", "shsp", c_hsp)
                        p.wait("act", "dve", c_comb_done)
                        for gi_ in range(4):
                            p.wait("act", xn_group_done[gi_][0], xn_group_done[gi_][1])
                    if half == 1 and e_ == 0:
                        p.wait("act", "sst", p.c("sst"))
                        p.wait("act", "svd0", p.c("svd0"))
                        p.wait("act", "svd1", p.c("svd1"))
                        p.wait("act", "dve", p.c("dve"))
                w1src = w1b[:, slot, :, :] if pfbuf is None else pfbuf
                for m_in in range(4):
                    sls = [quad.acquire(), quad.acquire()]
                    for dc in range(DC):
                        for n in range(2):
                            sl, sn = sls[n]
                            fn = lambda e, sl=sl, dc=dc, n=n, m_in=m_in, w1src=w1src: e.matmul(
                                ps_q[:, sl, :], w1src[:, dc, m_in * 128:(m_in + 1) * 128],
                                xn_half[:, dc, n * 512:(n + 1) * 512],
                                start=(dc == 0), stop=(dc == DC - 1))
                            if dc == DC - 1 and n == 1:
                                cpe = p.inc("pe", fn, "pe", 1)
                            else:
                                p.do("pe", fn)
                    st3["mm1_pe_last"] = cpe
                    hc = s8 * 4 + m_in
                    p.wait("act", "pe", cpe)
                    if gi > 0:
                        p.wait("act", "pe", st3["mm2_gate"])
                    for n in range(2):
                        sl, sn = sls[n]
                        p.inc("act", lambda e, sl=sl, hc=hc, n=n, e_=e_: e.activation(
                            sh[:, hc, n * 512:(n + 1) * 512], ps_q[:, sl, :], AF.Silu,
                            bias=b1_pp[:, e_, hc:hc + 1]), sn, 1)
                st3["w1_pe_done"][slot] = st3["mm1_pe_last"]
            c_silu_marker = p.marker("act")
            if late_hook is not None and e_ == NE - 1:
                late_hook()

            if e_ == 0:
                emit_bias_init()
                c_re_dve = emit_Re(0)
                if half == 0 and not st3["casts_emitted"]:
                    st3["casts_emitted"] = True
                    st3["c_wq_cast"] = p.inc("pool", lambda e: e.dma_start(out=wq_bf.ap(), in_=d_qkvw), "swq", 16)
                    st3["c_wp_cast"] = p.inc("pool", lambda e: e.dma_start(out=wp_bf.ap(), in_=d_projw), "swp", 16)

            for mp in range(4):
                wslot = (gi * 4 + mp) % 2
                sem_w2 = "sw2%d" % wslot
                if not (gi == 0 and mp < 1):
                    if st3["w2_uses"][wslot] >= 1:
                        p.wait("pool", "pe", st3["w2_pe_done"][wslot])
                    if half == 1 and e_ == 0 and wslot == 1:
                        p.wait("pool", "pe", st3["k_readers_pe"])
                    p.inc("pool", lambda e, e_=e_, mp=mp, wslot=wslot: e.dma_start(
                        out=w2bufs[wslot][:, :, :],
                        in_=d_w2[e_, :, mp * 256:(mp + 1) * 256].rearrange("(c p) d -> p c d", p=128)),
                        sem_w2, 16)
                    st3["w2_uses"][wslot] += 1
                p.wait("pe", sem_w2, 16 * st3["w2_uses"][wslot])
                if mp == 0:
                    p.wait("pe", "act", c_silu_marker)
                for m_in in range(2):
                    m = mp * 2 + m_in
                    if mp == 3 and m_in == 1:
                        st3["pse_uses"] += 1
                        if st3["pse_uses"] >= 2:
                            p.wait("pe", "pse", 2 * (st3["pse_uses"] - 1))
                        tgt = [(psE[0], "pse"), (psE[1], "pse")]
                    else:
                        sls = [quad.acquire(), quad.acquire()]
                        tgt = [(ps_q[:, sls[n][0], :], sls[n][1]) for n in range(2)]
                    for hc in range(32):
                        for n in range(2):
                            fn = lambda e, t_=tgt[n][0], hc=hc, n=n, m_in=m_in, wslot=wslot: e.matmul(
                                t_, w2bufs[wslot][:, hc, m_in * 128:(m_in + 1) * 128],
                                sh[:, hc, n * 512:(n + 1) * 512],
                                start=(hc == 0), stop=(hc == 31))
                            if hc == 31 and n == 1:
                                cpe = p.inc("pe", fn, "pe", 1)
                            else:
                                p.do("pe", fn)
                    st3["mm2_gate"] = cpe
                    p.wait("dve", "pe", cpe)
                    p.wait("dve", "act", c_re_dve)
                    for n in range(2):
                        p.inc("dve", lambda e, t_=tgt[n][0], n=n: e.tensor_tensor(
                            ytmp[:, n, :], t_, R_e[:, n * 512:(n + 1) * 512],
                            ALU.mult), tgt[n][1], 1)
                    p.dr()
                    for n in range(2):
                        p.inc("dve", lambda e, m=m, n=n: e.tensor_tensor(
                            y_acc[:, m, n * 512:(n + 1) * 512],
                            y_acc[:, m, n * 512:(n + 1) * 512], ytmp[:, n, :], ALU.add), "dve", 1)
                        p.dr()
                st3["w2_pe_done"][wslot] = st3["mm2_gate"]

        if half == 1:
            p.wait("sp", "pe", st3["w2_pe_done"][0])
            p.wait("sp", "swp", st3["c_wp_cast"])
            for dc in range(DC):
                st3["c_pw"] = p.inc("sp", lambda e, dc=dc: e.dma_start(
                    out=pwT[:, dc, :], in_=wp_bf.ap()[:, dc * 128:(dc + 1) * 128],
                    transpose=True), "spw2", 16)

        c_yacc = p.marker("dve")
        st3["c_yacc"] = c_yacc
        p.wait("act", "dve", c_yacc)
        st3["g_silu"][half] = p.inc("act", lambda e: e.activation(
            y_acc[:, :, :], y_acc[:, :, :], AF.Silu), "act", 1)
        p.wait("sp", "act", st3["g_silu"][half])
        st3["g_store"][half] = p.inc("sp", lambda e, half=half: e.dma_start(
            out=g_dram.ap()[:, half * W:(half + 1) * W].rearrange("(c p) t -> p c t", p=128),
            in_=y_acc[:, :, :]), "sg", 16)

    emit_p3_half(0)

    p3h0_pe = p.c("pe")
    p.wait("sp", "swq", st3["c_wq_cast"])

    def load_wT(part, buf, pe_gate, semname):
        p.wait("sp", "pe", pe_gate)
        cs = 0
        for dc in range(DC):
            cs = p.inc("sp", lambda e, dc=dc, part=part, buf=buf: e.dma_start(
                out=buf[:, dc, :],
                in_=wq_bf.ap()[part * 1024:(part + 1) * 1024, dc * 128:(dc + 1) * 128],
                transpose=True), semname, 16)
        return (semname, cs)


    p.wait("dve", "act", st3["g_silu"][0])
    p.wait("dve", "sg", st3["g_store"][0])
    p.wait("dve", "sw20", p.c("sw20"))
    p.wait("dve", "sw21", p.c("sw21"))
    p.wait("dve", "sxh", 16 * st3["xh_loads"])

    qk_pe_last = {0: 0, 1: 0}
    wT_of = {0: wT, 1: wT2, 2: wT}
    c_wT_next = {0: load_wT(0, wT, max(st3["w1_pe_done"]), "swta")}
    p.wait("sp", "shsp", c_hsp)
    p.wait("sp", "pe", p3h0_pe)
    for ch in range(4):
        p.inc("sp", lambda e, ch=ch: e.dma_start(
            out=h2[:, :, ch * 512:(ch + 1) * 512],
            in_=h_dram.ap()[:, ch * 512:(ch + 1) * 512].rearrange("(c p) t -> p c t", p=128)),
            "sh2c%d" % ch, 16)
    p.wait("sp", "act", st3["g_silu"][0])
    p.wait("sp", "sg", st3["g_store"][0])
    c_wT_next[1] = load_wT(1, wT2, p3h0_pe, "swtb")
    for part, dst in ((0, q_tok), (1, k_tok)):
        c_wT = c_wT_next[part]
        for js in range(2):
            for tc in range(NT):
                sl, sn = quad.acquire()
                p.wait("pe", c_wT[0], c_wT[1])
                if part == 0 and tc % 4 == 0:
                    p.wait("pe", "sh2c%d" % (tc // 4), 16)
                wTb = wT_of[part]
                for dc in range(DC):
                    fn = lambda e, sl=sl, dc=dc, tc=tc, js=js, wTb=wTb: e.matmul(
                        ps_q[:, sl, :], h2[:, dc, tc * 128:(tc + 1) * 128],
                        wTb[:, dc, js * 512:(js + 1) * 512],
                        start=(dc == 0), stop=(dc == DC - 1))
                    if dc < DC - 1:
                        p.do("pe", fn)
                    else:
                        cpe = p.inc("pe", fn, "pe", 1)
                qk_pe_last[part] = cpe
                p.wait("dve", "pe", cpe)
                p.inc("dve", lambda e, sl=sl, tc=tc, js=js, dst=dst, part=part: e.scalar_tensor_tensor(
                    dst[:, tc, js * 512:(js + 1) * 512], ps_q[:, sl, :], 1.0,
                    rep4[:, part * 2 + js, :], ALU.mult, ALU.add), sn, 1)

    qk_marker = p.marker("dve")
    c_wT = load_wT(2, wT, qk_pe_last[0], "swta")
    for vjc in range(DC):
        for ts in range(4):
            sl, sn = quad.acquire()
            p.wait("pe", c_wT[0], c_wT[1])
            for dc in range(DC):
                fn = lambda e, sl=sl, dc=dc, vjc=vjc, ts=ts: e.matmul(
                    ps_q[:, sl, :], wT_of[2][:, dc, vjc * 128:(vjc + 1) * 128],
                    h2[:, dc, ts * 512:(ts + 1) * 512],
                    start=(dc == 0), stop=(dc == DC - 1))
                if dc < DC - 1:
                    p.do("pe", fn)
                else:
                    cpe = p.inc("pe", fn, "pe", 1)
            k_ = vjc * 4 + ts
            eslot = k_ % 2
            p.wait("dve", "pe", cpe)
            if k_ >= 2:
                p.wait("dve", "svd%d" % eslot, 16 * (k_ // 2))
            p.inc("dve", lambda e, sl=sl, vjc=vjc, eslot=eslot: e.tensor_scalar(
                vstage[:, eslot, :], ps_q[:, sl, :],
                qkvb_v[:, vjc:vjc + 1], None, ALU.add), sn, 1)
            p.dr()
            cv2 = p.marker("dve")
            p.wait("sp", "dve", cv2)
            p.inc("sp", lambda e, vjc=vjc, ts=ts, eslot=eslot: e.dma_start(
                out=v_dram.ap()[vjc * 128:(vjc + 1) * 128, ts * 512:(ts + 1) * 512],
                in_=vstage[:, eslot, :]), "svd%d" % eslot, 16)

    st3["v_pe_last"] = cpe
    p.wait("pool", "pe", st3["v_pe_last"])
    for j in range(2):
        p.inc("pool", lambda e, j=j: e.dma_start(
            out=pfA[:, j, :, :],
            in_=d_w1[0, :, (2 + j) * 512:(3 + j) * 512].rearrange("(c p) h -> p c h", p=128)),
            "spfa", 16)
    for j in range(2):
        p.inc("pool", lambda e, j=j: e.dma_start(
            out=pfC[:, j, :, :],
            in_=d_w1[0, :, (4 + j) * 512:(5 + j) * 512].rearrange("(c p) h -> p c h", p=128)),
            "spfc", 16)
    p.wait("pool", "sg", st3["g_store"][0])
    p.wait("pool", "dve", st3["c_yacc"])
    for j in range(2):
        p.inc("pool", lambda e, j=j: e.dma_start(
            out=pfB[:, j, :, :],
            in_=d_w1[0, :, (6 + j) * 512:(7 + j) * 512].rearrange("(c p) h -> p c h", p=128)),
            "spfb", 16)
    p.wait("pe", "dve", qk_marker)
    s_store = 0
    scores_last = 0
    for qfc in range(DC):
        for ks in range(2):
            sl, sn = quad.acquire()
            for tc in range(NT):
                fn = lambda e, sl=sl, tc=tc, qfc=qfc, ks=ks: e.matmul(
                    ps_q[:, sl, :], q_tok[:, tc, qfc * 128:(qfc + 1) * 128],
                    k_tok[:, tc, ks * 512:(ks + 1) * 512],
                    start=(tc == 0), stop=(tc == NT - 1))
                if tc < NT - 1:
                    p.do("pe", fn)
                else:
                    cpe = p.inc("pe", fn, "pe", 1)
            scores_last = cpe
            p.wait("act", "pe", cpe)
            if qfc >= 1:
                p.wait("act", "sst", s_store)
            p.inc("act", lambda e, sl=sl, ks=ks: e.activation(
                S_stage[:, ks * 512:(ks + 1) * 512], ps_q[:, sl, :], AF.Copy,
                scale=SCALE_S), sn, 1)
        c_sev = p.marker("act")
        p.wait("sp", "act", c_sev)
        s_store = p.inc("sp", lambda e, qfc=qfc: e.dma_start(
            out=S_dram.ap()[qfc * 128:(qfc + 1) * 128, :], in_=S_stage[:, :]), "sst", 16)
    st3["k_readers_pe"] = scores_last

    p.wait("pool", "sst", s_store)
    p.inc("pool", lambda e: e.collective_compute(
        "AllReduce", ALU.add, replica_groups=groups,
        ins=[S_dram.ap()], outs=[Ssum_dram.ap()]), "cc", 1)

    if "q" in tap_aps:
        p.wait("pool", "dve", qk_marker)
        p.inc("pool", lambda e: e.dma_start(out=tap_aps["q"], in_=q_tok[:, :, :]), "stapp", 16)
    if "k" in tap_aps:
        p.wait("pool", "dve", qk_marker)
        p.inc("pool", lambda e: e.dma_start(out=tap_aps["k"], in_=k_tok[:, :, :]), "stapp", 16)
    if "v" in tap_aps:
        p.wait("pool", "svd0", p.c("svd0"))
        p.wait("pool", "svd1", p.c("svd1"))
        p.inc("pool", lambda e: e.dma_start(out=tap_aps["v"], in_=v_dram.ap()), "stapp", 16)
    if "Spart" in tap_aps:
        p.wait("pool", "sst", s_store)
        p.inc("pool", lambda e: e.dma_start(out=tap_aps["Spart"], in_=S_dram.ap()), "stapp", 16)
    if "Ssum" in tap_aps:
        p.wait("pool", "cc", 1)
        p.inc("pool", lambda e: e.dma_start(out=tap_aps["Ssum"], in_=Ssum_dram.ap()), "stapp", 16)

    if phase_limit < 4:
        return finish(nc, p, tap_aps)

    emit_p3_half(1, hook_after={
        1: lambda: emit_softmax_qfc([0, 1]),
        2: lambda: emit_softmax_qfc([2, 3]),
        3: lambda: emit_softmax_qfc([4, 5]),
        4: lambda: emit_softmax_qfc([6, 7]),
    }, late_hook=emit_watt_loads)

    if "g" in tap_aps:
        p.wait("sp", "sg", p.c("sg"))
        p.inc("sp", lambda e: e.dma_start(out=tap_aps["g"], in_=g_dram.ap()), "stap", 16)

    if phase_limit < 5:
        return finish(nc, p, tap_aps)

    p3_pe_end = p.c("pe")
    p3_dve_end = p.c("dve")
    p.wait("sp", "pe", p3_pe_end)
    p.wait("sp", "dve", st3["c_yacc"])
    p.wait("sp", "cc", 1)

    c_vl = {}
    c_vl[0] = p.inc("sp", lambda e: e.dma_start(
        out=v2[:, :, 0:512],
        in_=v_dram.ap()[:, 0:512].rearrange("(c p) t -> p c t", p=128)), "svl0", 16)
    c_pw = st3["c_pw"]
    p.inc("sp", lambda e: e.dma_start(
        out=x_stage[:, :, :],
        in_=dx[0:512, :].rearrange("(c p) d -> p c d", p=128)), "sxr", 16)
    for cs in range(1, 4):
        c_vl[cs] = p.inc("sp", lambda e, cs=cs: e.dma_start(
            out=v2[:, :, cs * 512:(cs + 1) * 512],
            in_=v_dram.ap()[:, cs * 512:(cs + 1) * 512].rearrange("(c p) t -> p c t", p=128)),
            "svl%d" % cs, 16)

    o_store = {}
    sg_done = {}
    a_done_cs = {}
    c_trs_prev = 0
    for cs in range(4):
        for qfc in range(DC):
            if qfc == 0:
                p.wait("pe", "svl%d" % cs, c_vl[cs])
                if cs == 0:
                    p.wait("pe", "swat", st3["c_wat"])
            sl, sn = quad.acquire()
            for kfc in range(DC):
                fn = lambda e, sl=sl, kfc=kfc, qfc=qfc, cs=cs: e.matmul(
                    ps_q[:, sl, :], w_attT[:, kfc, qfc * 128:(qfc + 1) * 128],
                    v2[:, kfc, cs * 512:(cs + 1) * 512],
                    start=(kfc == 0), stop=(kfc == DC - 1))
                if kfc < DC - 1:
                    p.do("pe", fn)
                else:
                    cpe = p.inc("pe", fn, "pe", 1)
            p.wait("act", "pe", cpe)
            p.inc("act", lambda e, sl=sl, qfc=qfc, cs=cs: e.activation(
                a_fm[:, qfc, cs * 512:(cs + 1) * 512], ps_q[:, sl, :], AF.Copy), sn, 1)
        a_done_cs[cs] = p.marker("act")

        if cs >= 1:
            p.wait("sp", "sout", o_store[cs - 1])
            p.wait("sp", "dve", p.c("dve"))
            p.inc("sp", lambda e, cs=cs: e.dma_start(
                out=x_stage[:, :, :],
                in_=dx[cs * 512:(cs + 1) * 512, :].rearrange("(c p) d -> p c d", p=128)), "sxr", 16)
        def emit_tr(pend):
            m_, sgslot, cd_sg_ = pend
            s32, n32 = t32m.acquire()
            p.wait("pe", "pool", cd_sg_)
            for j in range(4):
                fn = lambda e, j=j, sgslot=sgslot: e.transpose(
                    ps_t32[:, j, :], sgbuf[:, sgslot, j * 128:(j + 1) * 128], ident_f[:, :])
                if j < 3:
                    p.do("pe", fn)
                else:
                    c_trs = p.inc("pe", fn, "pe", 1)
            p.wait("dve", "pe", c_trs)
            p.wait("dve", "sxr", 16 * (cs + 1))
            for j in range(4):
                last = (j == 3)
                p.inc("dve", lambda e, j=j, m_=m_: e.tensor_tensor(
                    o_stage[:, j, m_ * 128:(m_ + 1) * 128], ps_t32[:, j, :],
                    x_stage[:, j, m_ * 128:(m_ + 1) * 128], ALU.add), n32 if last else "dve", 1)
            p.dr()
            return c_trs

        pend = None
        for m in range(DC):
            sl, sn = quad.acquire()
            p.wait("pe", "act", a_done_cs[cs])
            if cs == 0 and m == 0:
                p.wait("pe", "spw2", c_pw)
            for qc in range(DC):
                fn = lambda e, sl=sl, qc=qc, m=m, cs=cs: e.matmul(
                    ps_q[:, sl, :], pwT[:, qc, m * 128:(m + 1) * 128],
                    a_fm[:, qc, cs * 512:(cs + 1) * 512],
                    start=(qc == 0), stop=(qc == DC - 1))
                if qc < DC - 1:
                    p.do("pe", fn)
                else:
                    cpe = p.inc("pe", fn, "pe", 1)
            k_ep = cs * DC + m
            eslot = k_ep % 2
            sgslot = k_ep % 2
            if k_ep >= 2:
                p.wait("sp", "pool", sg_done[k_ep - 2])
            p.inc("sp", lambda e, cs=cs, m=m, eslot=eslot: e.dma_start(
                out=xn_t[:, eslot, :], in_=xn_dram.ap()[m * 128:(m + 1) * 128, cs * 512:(cs + 1) * 512]),
                "sxnl%d" % eslot, 16)
            if cs == 0 and m == 0:
                p.wait("sp", "sg", st3["g_store"][0])
            if cs == 2 and m == 0:
                p.wait("sp", "sg", p.c("sg"))
            p.inc("sp", lambda e, cs=cs, m=m, eslot=eslot: e.dma_start(
                out=g_t[:, eslot, :], in_=g_dram.ap()[m * 128:(m + 1) * 128, cs * 512:(cs + 1) * 512]),
                "sgl%d" % eslot, 16)
            p.wait("dve", "pe", cpe)
            if k_ep >= 1:
                p.wait("dve", "pool", sg_done.get(k_ep - 1, 0))
            cd_pt = p.inc("dve", lambda e, sl=sl, m=m: e.tensor_scalar(
                ptmp[:, :], ps_q[:, sl, :], projb_pp[:, m:m + 1], None, ALU.add), sn, 1)
            c_ptm = p.marker("dve")
            p.wait("pool", "dve", c_ptm)
            p.wait("pool", "sxnl%d" % eslot, 16 * ((k_ep // 2) + 1))
            p.wait("pool", "sgl%d" % eslot, 16 * ((k_ep // 2) + 1))
            p.inc("pool", lambda e, eslot=eslot: e.tensor_tensor(
                ptmp[:, :], ptmp[:, :], xn_t[:, eslot, :], ALU.add), "pool", 1)
            p.wait("pool", "pool", p.c("pool"))
            if k_ep >= 2:
                p.wait("pool", "pe", c_trs_prev)
            cd_sg = p.inc("pool", lambda e, eslot=eslot, sgslot=sgslot: e.tensor_tensor(
                sgbuf[:, sgslot, :], ptmp[:, :], g_t[:, eslot, :], ALU.mult), "pool", 1)
            sg_done[k_ep] = cd_sg
            if pend is not None:
                c_trs_prev = emit_tr(pend)
            pend = (m, sgslot, cd_sg)
        c_trs_prev = emit_tr(pend)
        pend = None
        c_om = p.marker("dve")
        p.wait("sp", "dve", c_om)
        o_store[cs] = p.inc("sp", lambda e, cs=cs: e.dma_start(
            out=d_out[cs * 512:(cs + 1) * 512, :].rearrange("(c p) d -> p c d", p=128),
            in_=o_stage[:, :, :]), "sout", 16)

    return finish(nc, p, tap_aps)


def finish(nc, p, tap_aps):
    for sname in list(p.cnt):
        if p.cnt[sname] > 0 and sname != "sp":
            p.wait("sp", sname, p.cnt[sname])
    with nc.Block() as block:
        @block.tensor
        def _(e):
            for f in p.ops["pe"]:
                f(e)

        @block.scalar
        def _(e):
            for f in p.ops["act"]:
                f(e)

        @block.vector
        def _(e):
            for f in p.ops["dve"]:
                f(e)

        @block.gpsimd
        def _(e):
            for f in p.ops["pool"]:
                f(e)

        @block.sync
        def _(e):
            for f in p.ops["sp"]:
                f(e)
    return nc


def kernel(L0_attn_x, L0_attn_norm_w, L0_attn_norm_b, L0_attn_qkv_w, L0_attn_qkv_b,
           L0_attn_proj_w, L0_attn_proj_b, L0_ffn_gate_w, L0_ffn_gate_b,
           L0_ffn_w1, L0_ffn_b1, L0_ffn_w2, L0_ffn_b2):
    from concourse.bass_utils import run_bass_kernel_spmd
    x = np.ascontiguousarray(np.asarray(L0_attn_x, dtype=np.float32))
    weights = {
        "norm_w": np.ascontiguousarray(np.asarray(L0_attn_norm_w, np.float32)),
        "norm_b": np.ascontiguousarray(np.asarray(L0_attn_norm_b, np.float32)),
        "qkv_w": np.ascontiguousarray(np.asarray(L0_attn_qkv_w, np.float32)),
        "qkv_b": np.ascontiguousarray(np.asarray(L0_attn_qkv_b, np.float32)),
        "proj_w": np.ascontiguousarray(np.asarray(L0_attn_proj_w, np.float32)),
        "proj_b": np.ascontiguousarray(np.asarray(L0_attn_proj_b, np.float32)),
        "gate_w": np.ascontiguousarray(np.asarray(L0_ffn_gate_w, np.float32)),
        "gate_b": np.ascontiguousarray(np.asarray(L0_ffn_gate_b, np.float32)),
        "w1": np.ascontiguousarray(np.asarray(L0_ffn_w1, np.float32)),
        "b1": np.ascontiguousarray(np.asarray(L0_ffn_b1, np.float32)),
        "w2": np.ascontiguousarray(np.asarray(L0_ffn_w2, np.float32)),
        "b2": np.ascontiguousarray(np.asarray(L0_ffn_b2, np.float32)),
    }
    nc = build_program(groups=[[0, 1, 2, 3], [4, 5, 6, 7]])
    in_maps = []
    for c in range(8):
        n, part = c // 4, c % 4
        m = {"x": np.ascontiguousarray(x[n, part * T:(part + 1) * T, :])}
        m.update(weights)
        in_maps.append(m)
    res = run_bass_kernel_spmd(nc, in_maps, core_ids=list(range(8)))
    out = np.empty((2, 8192, 1024), dtype=np.float32)
    for c in range(8):
        n, part = c // 4, c % 4
        out[n, part * T:(part + 1) * T, :] = res.results[c]["out"]
    return out


# version 23
# speedup vs baseline: 1.0338x; 1.0338x over previous
import numpy as np
import concourse.bass as bass
import concourse.mybir as mybir

F32 = mybir.dt.float32
BF16 = mybir.dt.bfloat16
AF = mybir.ActivationFunctionType
ALU = mybir.AluOpType
AX = mybir.AxisListType.X

T = 2048
D = 1024
HF = 4096
NE = 8
NT = T // 128
DC = D // 128
W = 1024
EPS = 1e-5
SCALE_S = float(1.0 / np.sqrt(8192.0))

ENGS = ("pe", "act", "dve", "pool", "sp")
DBG = False


class Plan:
    def __init__(self, nc):
        self.nc = nc
        self.ops = {e: [] for e in ENGS}
        self.sems = {}
        self.cnt = {}

    def sem(self, name):
        if name not in self.sems:
            self.sems[name] = self.nc.alloc_semaphore(name)
            self.cnt[name] = 0
        return self.sems[name]

    def do(self, eng, fn):
        self.ops[eng].append(fn)

    def inc(self, eng, fn, sname, amt):
        s = self.sem(sname)
        self.cnt[sname] += amt

        def g(e, fn=fn, s=s, amt=amt):
            fn(e).then_inc(s, amt)

        self.ops[eng].append(g)
        return self.cnt[sname]

    def wait(self, eng, sname, val):
        if val <= 0:
            return
        s = self.sem(sname)
        self.ops[eng].append(lambda e, s=s, v=val: e.wait_ge(s, v))

    def c(self, sname):
        return self.cnt.get(sname, 0)

    def marker(self, eng):
        return self.inc(eng, lambda e: e.drain(), eng, 1)

    def dr(self, eng="dve"):
        self.ops[eng].append(lambda e: e.drain())


class SlotMgr:
    def __init__(self, plan, n, sem_prefix):
        self.plan = plan
        self.n = n
        self.names = [f"{sem_prefix}{i}" for i in range(n)]
        self.uses = [0] * n
        self.i = 0

    def acquire(self, writer_eng="pe"):
        s = self.i % self.n
        self.i += 1
        if self.uses[s] >= 1:
            self.plan.wait(writer_eng, self.names[s], self.uses[s])
        self.uses[s] += 1
        return s, self.names[s]


def build_program(taps=None, phase_limit=99, groups=None):
    taps = taps or {}
    groups = groups or [[0, 1, 2, 3], [4, 5, 6, 7]]
    nc = bass.Bass("TRN2", target_bir_lowering=False)
    p = Plan(nc)
    from concourse.bass_types import AP as _AP0

    dx = nc.dram_tensor("x", [T, D], F32, kind="ExternalInput").ap()
    d_nw = nc.dram_tensor("norm_w", [D], F32, kind="ExternalInput").ap()
    d_nb = nc.dram_tensor("norm_b", [D], F32, kind="ExternalInput").ap()
    d_qkvw = nc.dram_tensor("qkv_w", [3 * D, D], F32, kind="ExternalInput").ap()
    d_qkvb = nc.dram_tensor("qkv_b", [3 * D], F32, kind="ExternalInput").ap()
    d_projw = nc.dram_tensor("proj_w", [D, D], F32, kind="ExternalInput").ap()
    d_projb = nc.dram_tensor("proj_b", [D], F32, kind="ExternalInput").ap()
    d_gw = nc.dram_tensor("gate_w", [D, NE], F32, kind="ExternalInput").ap()
    d_gb = nc.dram_tensor("gate_b", [NE], F32, kind="ExternalInput").ap()
    d_w1 = nc.dram_tensor("w1", [NE, D, HF], F32, kind="ExternalInput").ap()
    d_b1 = nc.dram_tensor("b1", [NE, HF], F32, kind="ExternalInput").ap()
    d_w2 = nc.dram_tensor("w2", [NE, HF, D], F32, kind="ExternalInput").ap()
    d_b2 = nc.dram_tensor("b2", [NE, D], F32, kind="ExternalInput").ap()
    d_out = nc.dram_tensor("out", [T, D], F32, kind="ExternalOutput").ap()

    wq_bf = nc.dram_tensor("wq_bf", [3 * D, D], BF16)
    wp_bf = nc.dram_tensor("wp_bf", [D, D], BF16)
    xn_dram = nc.dram_tensor("xn_dram", [D, T], F32)
    g_dram = nc.dram_tensor("g_dram", [D, T], F32)
    v_dram = nc.dram_tensor("v_dram", [D, T], BF16)
    h_dram = nc.dram_tensor("h_dram", [D, T], BF16)
    S_dram = nc.dram_tensor("S_dram", [D, D], F32)
    Ssum_dram = nc.dram_tensor("Ssum_dram", [D, D], F32)
    watt_dram = nc.dram_tensor("watt_dram", [D, D], BF16)

    tap_aps = {}
    for tname, tshape in taps.items():
        tap_aps[tname] = nc.dram_tensor("tap_" + tname, list(tshape), F32,
                                        kind="ExternalOutput").ap()

    ps_q = nc.alloc_psum_tensor("ps_q", [128, 4, 512], F32)
    ps_t32 = nc.alloc_psum_tensor("ps_t32", [128, 8, 128], F32)
    ps_tb = nc.alloc_psum_tensor("ps_tb", [128, 8, 128], BF16)
    ps_x = nc.alloc_psum_tensor("ps_x", [128, 512], F32)

    quad = SlotMgr(p, 4, "q")
    t32m = SlotMgr(p, 1, "t32_")
    tbm = SlotMgr(p, 1, "tb_")
    pxm = SlotMgr(p, 1, "px_")

    SB0 = 16512

    def at(name, shape, dt, off):
        return nc.alloc_sbuf_tensor_at(name, shape, dt, offset=off)

    po = [SB0]

    def pal(name, shape, dt):
        nb = int(np.prod(shape[1:])) * (2 if dt == BF16 else 4)
        off = (po[0] + 31) // 32 * 32
        po[0] = off + nb
        assert po[0] <= SB0 + 30000
        return at(name, shape, dt, off)

    ident_bf = pal("ident_bf", [128, 128], BF16)
    ident_f = pal("ident_f", [128, 128], F32)
    nw_rep = pal("nw_rep", [128, D], F32)
    nb_rep = pal("nb_rep", [128, D], F32)
    qkvb_v = pal("qkvb_v", [128, DC], F32)
    projb_pp = pal("projb_pp", [128, DC], F32)
    b1_pp = pal("b1_pp", [128, NE, HF // 128], F32)
    b2_lhs = pal("b2_lhs", [NE, D], F32)
    gate_w_sb = pal("gate_w_sb", [128, DC, NE], F32)
    gate_b_pp = pal("gate_b_pp", [NE, 4], F32)
    iota8 = pal("iota8", [128, NE], F32)
    ones_col = pal("ones_col", [1, 128], F32)
    stats = pal("stats", [128, 4, 16], F32)
    rowbuf = pal("rowbuf", [1, 1024], F32)
    poolscr = pal("poolscr", [1, 128], F32)
    eps_col = pal("eps_col", [128, 1], F32)
    prb0 = pal("prb0", [128, 1], F32)
    stage8 = pal("stage8", [NE, 128], F32)
    rep4 = pal("rep4", [128, 4, 512], F32)
    ones_bf = pal("ones_bf", [1, 128], BF16)

    sel8 = at("sel8", [NE, NE, 128], BF16, 33792)
    sstage2 = at("sstage2", [128, 2, 1024], F32, 17280)
    watt_row = at("watt_row", [128, 2, 1024], BF16, 36928)

    comb_bf = at("comb_bf", [NE, T], BF16, 46528)
    b2_bf = at("b2_bf", [NE, D], BF16, 50624)
    rowbf = at("rowbf", [1, 1024], BF16, 52672)
    h_fm = at("h_fm", [128, DC, T], BF16, 54720)
    glog = at("glog", [NE, T], F32, 87488)
    x_sb = at("x_sb", [128, 3, D], F32, 95680)
    xn_tok2 = at("xn_tok2", [128, 2, D], F32, 107968)
    ztmp2 = at("ztmp2", [128, 2, D], F32, 116160)
    h_tok2 = at("h_tok2", [128, 2, D], BF16, 124352)
    xn_stage = at("xn_stage", [128, 2, DC, 512], F32, 128448)
    p15b = at("p15b", [128, 8, NT, NE], F32, 95680)
    p15s = at("p15s", [128, 8, NT], F32, 99808)
    p15dbg = at("p15dbg", [128, NT, 24], F32, 100352)
    sh = at("sh", [128, 32, W], BF16, 54720)
    y_acc = at("y_acc", [128, DC, W], F32, 120256)
    R_e = at("R_e", [128, W], F32, 153024)
    ytmp = at("ytmp", [128, 2, 512], F32, 157120)
    w1b = at("w1b", [128, 2, DC, 512], BF16, 163264)
    w2bA = at("w2bA", [128, 32, 256], BF16, 179648)
    w2bB = at("w2bB", [128, 32, 256], BF16, 196032)
    xn_half = at("xn_half", [128, DC, W], BF16, 212416)
    w2bufs = [w2bA, w2bB]
    wT = at("wT", [128, DC, 1024], BF16, 163264)
    wT2 = at("wT2", [128, DC, 1024], BF16, 144832)
    S_stage = at("S_stage", [128, 1024], F32, 71104)
    vstage = at("vstage", [128, 2, 512], BF16, 75200)
    rep_js = at("rep_js", [128, 512], F32, 77248)
    h2 = at("h2", [128, DC, T], BF16, 79296)
    pfA = at("pfA", [128, 2, DC, 512], BF16, 79296)
    pfC = at("pfC", [128, 2, DC, 512], BF16, 95680)
    pfB = at("pfB", [128, 2, DC, 512], BF16, 144832)
    q_tok = at("q_tok", [128, NT, 1024], BF16, 112064)
    k_tok = at("k_tok", [128, NT, 1024], BF16, 196032)
    P4B = 46528
    v2 = at("v2", [128, DC, T], BF16, P4B)
    sgbuf = at("sgbuf", [128, 2, 512], F32, 79296)
    a_fm = at("a_fm", [128, DC, T], BF16, 83392)
    xn_t = at("xn_t", [128, 2, 512], F32, 153024)
    g_t = at("g_t", [128, 2, 512], F32, 157120)
    ptmp = at("ptmp", [128, 512], F32, 161216)
    w_attT = at("w_attT", [128, DC, 1024], BF16, 163264)
    pwT = at("pwT", [128, DC, 1024], BF16, 179648)
    x_stage = at("x_stage", [128, 4, 1024], F32, 196032)
    o_stage = at("o_stage", [128, 4, 1024], F32, 212416)

    p.inc("pool", lambda e: e.iota(iota8[:, :], pattern=[[1, NE]], base=0,
                                   channel_multiplier=0,
                                   allow_small_or_imprecise_dtypes=True), "pool", 1)
    p.inc("pool", lambda e: e.memset(ident_bf[:, :], 0.0), "pool", 1)
    p.wait("pool", "pool", p.c("pool"))
    p.inc("pool", lambda e: e.affine_select(out=ident_bf[:, :], in_=ident_bf[:, :],
                                            compare_op=ALU.not_equal, fill=1.0, base=0,
                                            pattern=[[-1, 128]], channel_multiplier=1), "pool", 1)
    p.inc("pool", lambda e: e.memset(ident_f[:, :], 0.0), "pool", 1)
    p.wait("pool", "pool", p.c("pool"))
    p.inc("pool", lambda e: e.affine_select(out=ident_f[:, :], in_=ident_f[:, :],
                                            compare_op=ALU.not_equal, fill=1.0, base=0,
                                            pattern=[[-1, 128]], channel_multiplier=1), "pool", 1)
    p.wait("pool", "pool", p.c("pool"))
    c_pool_const = p.inc("pool", lambda e: e.tensor_copy(poolscr[0:1, :], ident_f[0:1, :]),
                         "pool", 1)

    p.inc("sp", lambda e: e.dma_start(out=b2_lhs[:, :], in_=d_b2), "sw0", 16)
    p.inc("sp", lambda e: e.dma_start(out=gate_w_sb[:, :, :],
                                      in_=d_gw.rearrange("(c p) e -> p c e", p=128)), "sw0", 16)
    p.inc("sp", lambda e: e.dma_start(out=gate_b_pp[:, 0:1], in_=d_gb[:, None]), "sw0", 16)
    c_sp_small = p.c("sw0")

    c_ones = p.inc("dve", lambda e: e.memset(ones_col[:, :], 1.0), "dve", 1)
    p.inc("dve", lambda e: e.memset(ones_bf[:, :], 1.0), "dve", 1)
    p.inc("dve", lambda e: e.memset(eps_col[:, :], EPS), "dve", 1)
    p.inc("dve", lambda e: e.memset(prb0[:, :], 0.0), "dve", 1)

    def rep_row(dst, src_row_load):
        p.wait("sp", "pe", p.c("pe"))
        c_row = p.inc("sp", src_row_load, "srow", 16)
        for ns in range(2):
            s_, sname = pxm.acquire()
            p.wait("pe", "srow", c_row)
            p.wait("pe", "dve", c_ones)
            cpe = p.inc("pe", lambda e, ns=ns: e.matmul(
                ps_x[:, :], ones_col[:, :], rowbuf[:, ns * 512:(ns + 1) * 512],
                start=True, stop=True), "pe", 1)
            p.wait("dve", "pe", cpe)
            p.inc("dve", lambda e, ns=ns, dst=dst: e.tensor_copy(
                dst[:, ns * 512:(ns + 1) * 512], ps_x[:, :]), sname, 1)
            p.dr()

    rep_row(nw_rep, lambda e: e.dma_start(out=rowbuf[0:1, :], in_=d_nw[None, :]))
    rep_row(nb_rep, lambda e: e.dma_start(out=rowbuf[0:1, :], in_=d_nb[None, :]))

    def build_rep4():
        for idx in range(4):
            base_j = (idx // 2) * 1024 + (idx % 2) * 512
            p.dr()
            p.wait("sp", "pe", p.c("pe"))
            c_row = p.inc("sp", lambda e, base_j=base_j: e.dma_start(
                out=rowbuf[0:1, 0:512], in_=d_qkvb[None, base_j:base_j + 512]), "srw2", 16)
            sx_, nx = pxm.acquire()
            p.wait("pe", "srw2", c_row)
            p.wait("pe", "dve", p.c("dve"))
            cpe = p.inc("pe", lambda e: e.matmul(ps_x[:, :], ones_col[:, :], rowbuf[0:1, 0:512],
                                                 start=True, stop=True), "pe", 1)
            p.wait("dve", "pe", cpe)
            p.inc("dve", lambda e, idx=idx: e.tensor_copy(rep4[:, idx, :], ps_x[:, :]), nx, 1)
            p.dr()
        return p.marker("dve")
    c_rep4 = build_rep4()
    p.wait("dve", "pe", p.c("pe"))
    p.wait("dve", "pool", p.c("pool"))
    for e8 in range(NE):
        src_ = ident_bf[0:NE, e8:e8 + 1]
        p.inc("dve", lambda e, e8=e8, src_=src_: e.tensor_copy(
            sel8[:, e8, :],
            _AP0(tensor=src_.tensor, offset=src_.offset,
                 ap=[list(src_.ap[0]), list(src_.ap[1]), [0, 128]])), "dve", 1)
    p.dr()


    b1_8p = at("b1_8p", [NE, HF], F32, 163264)
    c_b1 = p.inc("sp", lambda e: e.dma_start(out=b1_8p[:, :], in_=d_b1), "sb1", 16)
    p.wait("pe", "sb1", c_b1)
    p.wait("pe", "pool", c_pool_const)
    for hc in range(HF // 128):
        s_, sname = t32m.acquire()
        cpe = p.inc("pe", lambda e, hc=hc: e.transpose(
            ps_t32[:, 0, 0:NE], b1_8p[:, hc * 128:(hc + 1) * 128], ident_f[0:NE, 0:NE]),
            "pe", 1)
        p.wait("dve", "pe", cpe)
        p.inc("dve", lambda e, hc=hc: e.tensor_copy(b1_pp[:, :, hc], ps_t32[:, 0, 0:NE]),
              sname, 1)
    c_b1_rd = p.c("pe")
    for (src_ap, dst) in ((d_qkvb[2 * D:3 * D], qkvb_v), (d_projb, projb_pp)):
        p.wait("sp", "pe", p.c("pe"))
        c_ld = p.inc("sp", lambda e, src_ap=src_ap: e.dma_start(
            out=stage8[:, :], in_=src_ap.rearrange("(c p) -> c p", p=128)), "sst8", 16)
        s_, n32c = t32m.acquire()
        p.wait("pe", "sst8", c_ld)
        c_tr = p.inc("pe", lambda e: e.transpose(ps_t32[:, 0, 0:NE], stage8[:, :],
                                                 ident_f[0:NE, 0:NE]), "pe", 1)
        p.wait("dve", "pe", c_tr)
        p.inc("dve", lambda e, dst=dst: e.tensor_copy(dst[:, :], ps_t32[:, 0, 0:NE]), n32c, 1)
    p.wait("dve", "sw0", c_sp_small)
    p.inc("dve", lambda e: e.tensor_copy(b2_bf[:, :], b2_lhs[:, :]), "dve", 1)
    p.dr()
    c_b1_done = p.marker("dve")

    x_loads = {}
    for i0 in range(3):
        x_loads[i0] = p.inc("sp", lambda e, i0=i0: e.dma_start(
            out=x_sb[:, i0, :], in_=dx[i0 * 128:(i0 + 1) * 128, :]), "sx%d" % i0, 16)

    p.wait("pool", "dve", c_b1_done)
    sqrt_done = {}
    ts_done = {}
    ca_stage_of = {}
    ch_of = {}
    x_done = {}
    xn_group_done = {}
    tr32_done = {}
    trh_done = {}
    gate_done = {}
    glog_done = []
    for tc in range(NT):
        xv = x_sb[:, tc % 3, :]
        xn_tok = xn_tok2[:, tc % 2, :]
        ztmp = ztmp2[:, tc % 2, :]
        h_tok = h_tok2[:, tc % 2, :]
        p.wait("dve", "sx%d" % (tc % 3), 16 * (tc // 3 + 1))

        def ln_statsA(src, pidx):
            st_ = stats[:, pidx % 4, :]
            if pidx >= 4:
                p.wait("dve", "act", sqrt_done[pidx - 4])
                if pidx - 4 in ts_done:
                    p.wait("dve", "pool", ts_done[pidx - 4])
            p.inc("dve", lambda e, src=src, st_=st_: e.bn_stats(st_[:, 0:6], src[:, 0:512]), "dve", 1)
            p.inc("dve", lambda e, src=src, st_=st_: e.bn_stats(st_[:, 6:12], src[:, 512:1024]), "dve", 1)
            p.dr()
            p.inc("dve", lambda e, st_=st_: e.bn_aggr(st_[:, 12:14], st_[:, 0:12]), "dve", 1)
            p.dr()
            cd = p.c("dve")
            p.wait("act", "dve", cd)
            ca_sq = p.inc("act", lambda e, st_=st_: e.activation(st_[:, 14:15], st_[:, 13:14],
                                                                 AF.Sqrt, bias=eps_col[:, 0:1]), "act", 1)
            sqrt_done[pidx] = ca_sq

        def ln_finishB(src, dst, pidx):
            st_ = stats[:, pidx % 4, :]
            p.wait("dve", "act", sqrt_done[pidx])
            p.inc("dve", lambda e, st_=st_: e.reciprocal(st_[:, 15:16], st_[:, 14:15]), "dve", 1)
            p.inc("dve", lambda e, st_=st_: e.tensor_scalar(st_[:, 13:14], st_[:, 12:13], -1.0, None, ALU.mult), "dve", 1)
            p.dr()
            cdst = p.inc("dve", lambda e, src=src, dst=dst, st_=st_: e.tensor_scalar(
                dst, src, st_[:, 13:14], st_[:, 15:16], ALU.add, ALU.mult), "dve", 1)
            p.dr()
            return cdst

        if tc >= 2:
            p.wait("dve", "pe", tr32_done[tc - 2])
            p.wait("dve", "pool", ch_of[tc - 2])
            p.wait("pool", "pe", trh_done[tc - 2])
        if tc == 0:
            ln_statsA(xv, 0)
        cd_xn = ln_finishB(xv, xn_tok, 2 * tc)
        x_done[tc] = cd_xn
        if tc + 3 < NT:
            p.wait("sp", "dve", cd_xn)
            x_loads[tc + 3] = p.inc("sp", lambda e, tc=tc: e.dma_start(
                out=x_sb[:, tc % 3, :], in_=dx[(tc + 3) * 128:(tc + 4) * 128, :]),
                "sx%d" % (tc % 3), 16)

        ln_statsA(xn_tok, 2 * tc + 1)
        if tc + 1 < NT:
            xv_n = x_sb[:, (tc + 1) % 3, :]
            p.wait("dve", "sx%d" % ((tc + 1) % 3), 16 * ((tc + 1) // 3 + 1))
            ln_statsA(xv_n, 2 * (tc + 1))
        st_z = stats[:, (2 * tc + 1) % 4, :]
        p.wait("dve", "act", sqrt_done[2 * tc + 1])
        p.inc("dve", lambda e, st_z=st_z: e.reciprocal(st_z[:, 15:16], st_z[:, 14:15]), "dve", 1)
        p.inc("dve", lambda e, st_z=st_z: e.tensor_scalar(st_z[:, 13:14], st_z[:, 12:13], -1.0, None, ALU.mult), "dve", 1)
        p.dr()
        cdz_small = p.c("dve")
        p.wait("pool", "dve", cdz_small)
        ts_done[2 * tc + 1] = p.inc("pool", lambda e, st_z=st_z, xn_tok=xn_tok, ztmp=ztmp: e.tensor_scalar(
            ztmp, xn_tok, st_z[:, 13:14], st_z[:, 15:16], ALU.add, ALU.mult), "pool", 1)
        p.dr("pool")
        p.inc("pool", lambda e, ztmp=ztmp: e.tensor_tensor(ztmp, ztmp, nw_rep[:, :], ALU.mult), "pool", 1)
        p.dr("pool")
        ch = p.inc("pool", lambda e, ztmp=ztmp, h_tok=h_tok: e.tensor_tensor(h_tok, ztmp, nb_rep[:, :], ALU.add), "pool", 1)
        p.dr("pool")

        s32, n32 = t32m.acquire()
        p.wait("pe", "dve", cd_xn)
        for j in range(8):
            fn = lambda e, j=j, xn_tok=xn_tok: e.transpose(ps_t32[:, j, :], xn_tok[:, j * 128:(j + 1) * 128], ident_f[:, :])
            if j < 7:
                p.do("pe", fn)
            else:
                c_tr = p.inc("pe", fn, "pe", 1)
        tr32_done[tc] = c_tr
        if tc >= 1:
            emit_gate(tc - 1)
            emit_htr(tc - 1)
        if tc % 4 == 0 and tc >= 8:
            g_, cnt_ = xn_group_done[tc // 4 - 2]
            p.wait("act", g_, cnt_)
            p.wait("act", "pe", gate_done[(tc // 4 - 2) * 4 + 3])
        p.wait("act", "pe", c_tr)
        ca_stage = p.inc("act", lambda e, tc=tc: e.activation(
            xn_stage[:, (tc // 4) % 2, :, (tc % 4) * 128:(tc % 4) * 128 + 128],
            ps_t32[:, :, :], AF.Copy), "act", 1)
        ca_stage_of[tc] = ca_stage
        p.inc("act", lambda e: e.drain(), n32, 1)

        def emit_gate(tg):
            p.wait("pe", "act", ca_stage_of[tg])
            if tg == 0:
                p.wait("pe", "sw0", c_sp_small)
            sx_, nx = pxm.acquire()
            for dc in range(DC):
                fn = lambda e, dc=dc, tg=tg: e.matmul(
                    ps_x[0:NE, 0:128], gate_w_sb[:, dc, :],
                    xn_stage[:, (tg // 4) % 2, dc, (tg % 4) * 128:(tg % 4) * 128 + 128],
                    start=(dc == 0), stop=(dc == DC - 1))
                if dc < DC - 1:
                    p.do("pe", fn)
                else:
                    c_g = p.inc("pe", fn, "pe", 1)
            gate_done[tg] = c_g
            p.wait("dve", "pe", c_g)
            glog_done.append(p.inc("dve", lambda e, tg=tg: e.tensor_scalar(
                glog[:, tg * 128:(tg + 1) * 128], ps_x[0:NE, 0:128], gate_b_pp[:, 0:1],
                None, ALU.add), "dve", 1))
            p.inc("dve", lambda e: e.drain(), nx, 1)

        def emit_htr(tg):
            h_tg = h_tok2[:, tg % 2, :]
            sb_, nb_ = tbm.acquire()
            p.wait("pe", "pool", ch_of[tg])
            for j in range(8):
                fn = lambda e, j=j, h_tg=h_tg: e.transpose(ps_tb[:, j, :], h_tg[:, j * 128:(j + 1) * 128], ident_bf[:, :])
                if j < 7:
                    p.do("pe", fn)
                else:
                    c_trh = p.inc("pe", fn, "pe", 1)
            trh_done[tg] = c_trh
            p.wait("act", "pe", c_trh)
            p.inc("act", lambda e, tg=tg: e.activation(
                h_fm[:, :, tg * 128:(tg + 1) * 128], ps_tb[:, :, :], AF.Copy), "act", 1)
            p.inc("act", lambda e: e.drain(), nb_, 1)

        ch_of[tc] = ch
        if tc % 4 == 3:
            grp = tc // 4
            p.wait("sp", "act", ca_stage)
            xn_group_done[grp] = ("sxn%d" % (grp % 2), p.inc("sp", lambda e, grp=grp: e.dma_start(
                out=xn_dram.ap().rearrange("(c p) t -> p c t", p=128)[:, :, grp * 512:(grp + 1) * 512],
                in_=xn_stage[:, grp % 2, :, :]), "sxn%d" % (grp % 2), 16))

    emit_gate(NT - 1)
    emit_htr(NT - 1)
    c_p1_act = p.c("act")
    p.wait("sp", "act", c_p1_act)
    c_hsp = p.inc("sp", lambda e: e.dma_start(
        out=h_dram.ap().rearrange("(c p) t -> p c t", p=128), in_=h_fm[:, :, :]), "shsp", 16)

    if "xn_fm" in tap_aps:
        for gi_ in range(4):
            p.wait("sp", xn_group_done[gi_][0], xn_group_done[gi_][1])
        p.inc("sp", lambda e: e.dma_start(out=tap_aps["xn_fm"], in_=xn_dram.ap()), "stap", 16)
    if "glog" in tap_aps:
        p.wait("sp", "dve", p.c("dve"))
        p.inc("sp", lambda e: e.dma_start(out=tap_aps["glog"], in_=glog[:, :]), "stap", 16)
    if "h_fm" in tap_aps:
        p.wait("sp", "shsp", c_hsp)
        p.inc("sp", lambda e: e.dma_start(out=tap_aps["h_fm"], in_=h_dram.ap()), "stap", 16)

    if phase_limit < 2:
        return finish(nc, p, tap_aps)

    from concourse.bass_types import AP as _AP

    def bc3(ap2d, inner=NE):
        return _AP(tensor=ap2d.tensor, offset=ap2d.offset,
                   ap=[list(ap2d.ap[0]), list(ap2d.ap[1]), [0, inner]])

    def bcmid(ap2d, mid=NT):
        return _AP(tensor=ap2d.tensor, offset=ap2d.offset,
                   ap=[list(ap2d.ap[0]), [0, mid], list(ap2d.ap[1])])

    PT = p15b[:, 0, :, :]
    PL2 = p15b[:, 1, :, :]
    OH1 = p15b[:, 2, :, :]
    OH2 = p15b[:, 3, :, :]
    CMB = p15b[:, 4, :, :]
    NMX = p15s[:, 0, :]
    SSM = p15s[:, 1, :]
    M1 = p15s[:, 2, :]
    M2 = p15s[:, 3, :]
    I1 = p15s[:, 4, :]
    I2 = p15s[:, 5, :]
    WS = p15s[:, 6, :]
    IOTA3 = bcmid(iota8[:, :])

    tr_done = 0
    for tc in range(NT):
        s32, n32 = t32m.acquire()
        p.wait("pe", "dve", glog_done[tc])
        c_tr = p.inc("pe", lambda e, tc=tc: e.transpose(
            ps_t32[:, 0, 0:NE], glog[:, tc * 128:(tc + 1) * 128], ident_f[0:NE, 0:NE]), "pe", 1)
        p.wait("dve", "pe", c_tr)
        p.inc("dve", lambda e, tc=tc: e.tensor_copy(PT[:, tc, :], ps_t32[:, 0, 0:NE]), n32, 1)
        p.dr()

    def TT(out, a, b, op):
        p.inc("dve", lambda e, out=out, a=a, b=b, op=op: e.tensor_tensor(out, a, b, op), "dve", 1)
        p.dr()

    def TS(out, a, s1, s2, op0, op1=None):
        if op1 is None:
            p.inc("dve", lambda e, out=out, a=a, s1=s1, op0=op0: e.tensor_scalar(out, a, s1, None, op0), "dve", 1)
        else:
            p.inc("dve", lambda e, out=out, a=a, s1=s1, s2=s2, op0=op0, op1=op1: e.tensor_scalar(out, a, s1, s2, op0, op1), "dve", 1)
        p.dr()

    def RED(out, a, op, negate=False):
        p.inc("dve", lambda e, out=out, a=a, op=op, negate=negate: e.tensor_reduce(
            out, a, axis=AX, op=op, negate=negate), "dve", 1)
        p.dr()

    RED(NMX, PT, ALU.max, negate=True)
    TT(PL2, PT, bc3(NMX), ALU.add)
    cdve = p.c("dve")
    p.wait("act", "dve", cdve)
    ca_exp = p.inc("act", lambda e: e.activation(
        PL2.rearrange("p a b -> p (a b)"), PL2.rearrange("p a b -> p (a b)"), AF.Exp), "act", 1)
    p.wait("dve", "act", ca_exp)
    RED(SSM, PL2, ALU.add)
    p.inc("dve", lambda e: e.reciprocal(SSM, SSM), "dve", 1)
    p.dr()
    TT(PT, PL2, bc3(SSM), ALU.mult)
    RED(M1, PT, ALU.max)
    TT(PL2, PT, bc3(M1), ALU.is_ge)
    TT(OH1, IOTA3, PL2, ALU.mult)
    TS(PL2, PL2, -1.0, -9.0, ALU.add, ALU.mult)
    TT(OH1, OH1, PL2, ALU.add)
    RED(I1, OH1, ALU.min)
    TT(OH1, IOTA3, bc3(I1), ALU.is_equal)
    TT(PL2, OH1, bc3(M1), ALU.mult)
    TS(PL2, PL2, 1e9, None, ALU.mult)
    TT(PL2, PT, PL2, ALU.subtract)
    RED(M2, PL2, ALU.max)
    TT(PL2, PL2, bc3(M2), ALU.is_ge)
    TT(OH2, IOTA3, PL2, ALU.mult)
    TS(PL2, PL2, -1.0, -9.0, ALU.add, ALU.mult)
    TT(OH2, OH2, PL2, ALU.add)
    RED(I2, OH2, ALU.min)
    TT(OH2, IOTA3, bc3(I2), ALU.is_equal)
    TT(WS, M1, M2, ALU.add)
    TS(WS, WS, 1e-9, None, ALU.max)
    p.inc("dve", lambda e: e.reciprocal(WS, WS), "dve", 1)
    p.dr()
    TT(OH1, OH1, bc3(M1), ALU.mult)
    TT(OH2, OH2, bc3(M2), ALU.mult)
    TT(CMB, OH1, OH2, ALU.add)
    TT(CMB, CMB, bc3(WS), ALU.mult)
    ccomb = p.marker("dve")

    for tc in range(NT):
        s32b, n32b = t32m.acquire()
        p.wait("pe", "dve", ccomb)
        c_trc = p.inc("pe", lambda e, tc=tc: e.transpose(
            ps_t32[0:NE, 0, :], CMB[:, tc, :], ident_f[:, :]), "pe", 1)
        p.wait("dve", "pe", c_trc)
        p.inc("dve", lambda e, tc=tc: e.tensor_copy(comb_bf[:, tc * 128:(tc + 1) * 128], ps_t32[0:NE, 0, :]), n32b, 1)
        p.inc("dve", lambda e: e.drain(), "dve", 1)
    c_comb_done = p.c("dve")

    if "comb" in tap_aps:
        p.wait("sp", "dve", c_comb_done)
        p.inc("sp", lambda e: e.dma_start(out=tap_aps["comb"], in_=comb_bf[:, :]), "stap", 16)
    if "p15dbg" in tap_aps:
        p.wait("sp", "dve", p.c("dve"))
        p.inc("sp", lambda e: e.dma_start(out=tap_aps["p15dbg"], in_=p15dbg[:, :, :]), "stap", 16)

    if phase_limit < 3:
        return finish(nc, p, tap_aps)

    st3 = dict(w1_uses=[0, 0], w2_uses=[0, 0], w1_pe_done=[0, 0], w2_pe_done=[0, 0],
               mm1_pe_last=0, mm2_gate=0, g_silu={}, g_store={},
               casts_emitted=False, c_wq_cast=0, c_wp_cast=0, xh_loads=0,
               k_readers_pe=0)

    st3["exp_done"] = {}
    st3["watt_done"] = {}
    st3["watt_store"] = {}
    st3["pse_uses"] = 0

    def wide_psum(base):
        return _AP(tensor=base.tensor, offset=base.offset,
                   ap=[list(base.ap[0]), [1, 512]])
    psE = (wide_psum(ps_t32[:, 0, :]), wide_psum(ps_t32[:, 4, :]))

    st3["sst2_free"] = {}

    def emit_softmax_qfc(qfcs):
        cd_max = {}
        for qfc in qfcs:
            slot = qfc % 2
            if qfc == 0:
                p.wait("sp", "pool", ch_of[NT - 1])
                p.wait("sp", "cc", 1)
            if qfc >= 2:
                p.wait("sp", "dve", st3["sst2_free"][qfc - 2])
            p.inc("sp", lambda e, qfc=qfc, slot=slot: e.dma_start(
                out=sstage2[:, slot, :], in_=Ssum_dram.ap()[qfc * 128:(qfc + 1) * 128, :]),
                "ssl%d" % slot, 16)
        for qfc in qfcs:
            slot = qfc % 2
            st_ = stats[:, slot, :]
            p.wait("dve", "ssl%d" % slot, 16 * (qfc // 2 + 1))
            if qfc >= 2:
                p.wait("dve", "act", st3["exp_done"][qfc - 2])
            cd_max[qfc] = p.inc("dve", lambda e, slot=slot, st_=st_: e.reduce_max(
                st_[:, 0:1], sstage2[:, slot, :], axis=AX, negate=True), "dve", 1)
            p.dr()
        for qfc in qfcs:
            slot = qfc % 2
            st_ = stats[:, slot, :]
            p.wait("act", "dve", cd_max[qfc])
            st3["exp_done"][qfc] = p.inc("act", lambda e, slot=slot, st_=st_: e.activation(
                sstage2[:, slot, :], sstage2[:, slot, :], AF.Exp, bias=st_[:, 0:1]), "act", 1)
        for qfc in qfcs:
            slot = qfc % 2
            st_ = stats[:, slot, :]
            p.wait("dve", "act", st3["exp_done"][qfc])
            p.inc("dve", lambda e, slot=slot, st_=st_: e.reduce_sum(st_[:, 1:2], sstage2[:, slot, :], axis=AX), "dve", 1)
            p.dr()
            p.inc("dve", lambda e, st_=st_: e.reciprocal(st_[:, 2:3], st_[:, 1:2]), "dve", 1)
            p.dr()
            if qfc >= 2:
                p.wait("dve", "swr%d" % slot, 16 * (qfc // 2))
            p.inc("dve", lambda e, slot=slot, st_=st_: e.tensor_scalar(
                watt_row[:, slot, :], sstage2[:, slot, :], st_[:, 2:3], None, ALU.mult), "dve", 1)
            cm = p.marker("dve")
            st3["sst2_free"][qfc] = cm
            p.wait("sp", "dve", cm)
            st3["watt_store"][qfc] = p.inc("sp", lambda e, qfc=qfc, slot=slot: e.dma_start(
                out=watt_dram.ap()[qfc * 128:(qfc + 1) * 128, :], in_=watt_row[:, slot, :]),
                "swr%d" % slot, 16)

    def emit_watt_loads():
        p.wait("sp", "pe", st3["mm1_pe_last"])
        p.wait("sp", "swr0", 16 * 4)
        p.wait("sp", "swr1", 16 * 4)
        for kfc in range(DC):
            st3["c_wat"] = p.inc("sp", lambda e, kfc=kfc: e.dma_start(
                out=w_attT[:, kfc, :], in_=watt_dram.ap()[:, kfc * 128:(kfc + 1) * 128],
                transpose=True), "swat", 16)

    def emit_p3_half(half, hook_after=None, hook_at=2, late_hook=None):
        if half == 0:
            p.wait("pool", xn_group_done[0][0], xn_group_done[0][1])
            p.wait("pool", xn_group_done[1][0], xn_group_done[1][1])
        else:
            p.wait("pool", xn_group_done[2][0], xn_group_done[2][1])
            p.wait("pool", xn_group_done[3][0], xn_group_done[3][1])
            p.wait("pool", "pe", st3["mm1_pe_last"])
            p.wait("pool", "pe", st3["k_readers_pe"])
        for ch in range(4):
            st3["xh_loads"] += 1
            p.inc("pool", lambda e, half=half, ch=ch: e.dma_start(
                out=xn_half[:, ch * 2:(ch + 1) * 2, :],
                in_=xn_dram.ap()[ch * 256:(ch + 1) * 256,
                                 half * W:(half + 1) * W].rearrange("(c p) t -> p c t", p=128)),
                "sxh", 16)

        if half == 0:
            for gi_ in range(4):
                p.wait("dve", xn_group_done[gi_][0], xn_group_done[gi_][1])
        else:
            p.wait("dve", "act", st3["g_silu"][0])
            p.wait("dve", "sg", st3["g_store"][0])

        def emit_bias_init(half=half):
            p.wait("pe", "dve", c_comb_done)
            for m in range(DC):
                for n in range(2):
                    sl, sn = quad.acquire()
                    cpe = p.inc("pe", lambda e, sl=sl, m=m, n=n, half=half: e.matmul(
                        ps_q[:, sl, :], b2_bf[:, m * 128:(m + 1) * 128],
                        comb_bf[:, half * W + n * 512:half * W + (n + 1) * 512],
                        start=True, stop=True), "pe", 1)
                    p.wait("dve", "pe", cpe)
                    p.inc("dve", lambda e, sl=sl, m=m, n=n: e.tensor_copy(
                        y_acc[:, m, n * 512:(n + 1) * 512], ps_q[:, sl, :]), sn, 1)
            p.dr()

        def emit_Re(e_, half=half):
            p.wait("pe", "dve", c_comb_done)
            p.wait("act", "dve", p.c("dve"))
            for n in range(2):
                sx_, nx_ = pxm.acquire()
                cpe = p.inc("pe", lambda e, e_=e_, n=n, half=half: e.matmul(
                    ps_x[:, :], sel8[:, e_, :],
                    comb_bf[:, half * W + n * 512:half * W + (n + 1) * 512],
                    start=True, stop=True), "pe", 1)
                p.wait("act", "pe", cpe)
                p.inc("act", lambda e, n=n: e.activation(R_e[:, n * 512:(n + 1) * 512],
                                                         ps_x[:, :], AF.Copy), nx_, 1)
            return p.marker("act")

        for e_ in range(NE):
            gi = half * NE + e_
            if e_ > 0:
                c_re_dve = emit_Re(e_)
            if isinstance(hook_after, dict):
                if e_ in hook_after:
                    hook_after[e_]()
            elif hook_after is not None and e_ == hook_at:
                hook_after()

            if gi == 0:
                p.inc("pool", lambda e: e.dma_start(
                    out=w2bufs[0][:, :, :],
                    in_=d_w2[0, :, 0:256].rearrange("(c p) d -> p c d", p=128)),
                    "sw20", 16)
                st3["w2_uses"][0] += 1

            for s8 in range(8):
                slot = (gi * 8 + s8) % 2
                sem_w1 = "sw1%d" % slot
                pfbuf = None
                if gi == NE and s8 >= 2:
                    if s8 < 4:
                        pfbuf = pfA[:, s8 - 2, :, :]
                        p.wait("pe", "spfa", 32)
                    elif s8 < 6:
                        pfbuf = pfC[:, s8 - 4, :, :]
                        p.wait("pe", "spfc", 32)
                    else:
                        pfbuf = pfB[:, s8 - 6, :, :]
                        p.wait("pe", "spfb", 32)
                else:
                    if st3["w1_uses"][slot] >= 1:
                        p.wait("pool", "pe", st3["w1_pe_done"][slot])
                    if gi == 0 and s8 < 2:
                        p.wait("pool", "pe", c_b1_rd)
                    if gi == NE and s8 < 2:
                        p.wait("pool", "pe", st3["v_pe_last"])
                    p.inc("pool", lambda e, e_=e_, s8=s8, slot=slot: e.dma_start(
                        out=w1b[:, slot, :, :],
                        in_=d_w1[e_, :, s8 * 512:(s8 + 1) * 512].rearrange("(c p) h -> p c h", p=128)),
                        sem_w1, 16)
                    st3["w1_uses"][slot] += 1
                    p.wait("pe", sem_w1, 16 * st3["w1_uses"][slot])
                if s8 == 0:
                    p.wait("pe", "sxh", 16 * st3["xh_loads"])
                    if gi == 0:
                        p.wait("act", "shsp", c_hsp)
                        p.wait("act", "dve", c_comb_done)
                        for gi_ in range(4):
                            p.wait("act", xn_group_done[gi_][0], xn_group_done[gi_][1])
                    if half == 1 and e_ == 0:
                        p.wait("act", "sst", p.c("sst"))
                        p.wait("act", "svd0", p.c("svd0"))
                        p.wait("act", "svd1", p.c("svd1"))
                        p.wait("act", "dve", p.c("dve"))
                w1src = w1b[:, slot, :, :] if pfbuf is None else pfbuf
                for m_in in range(4):
                    sls = [quad.acquire(), quad.acquire()]
                    for dc in range(DC):
                        for n in range(2):
                            sl, sn = sls[n]
                            fn = lambda e, sl=sl, dc=dc, n=n, m_in=m_in, w1src=w1src: e.matmul(
                                ps_q[:, sl, :], w1src[:, dc, m_in * 128:(m_in + 1) * 128],
                                xn_half[:, dc, n * 512:(n + 1) * 512],
                                start=(dc == 0), stop=(dc == DC - 1))
                            if dc == DC - 1 and n == 1:
                                cpe = p.inc("pe", fn, "pe", 1)
                            else:
                                p.do("pe", fn)
                    st3["mm1_pe_last"] = cpe
                    hc = s8 * 4 + m_in
                    p.wait("act", "pe", cpe)
                    if gi > 0:
                        p.wait("act", "pe", st3["mm2_gate"])
                    for n in range(2):
                        sl, sn = sls[n]
                        p.inc("act", lambda e, sl=sl, hc=hc, n=n, e_=e_: e.activation(
                            sh[:, hc, n * 512:(n + 1) * 512], ps_q[:, sl, :], AF.Silu,
                            bias=b1_pp[:, e_, hc:hc + 1]), sn, 1)
                st3["w1_pe_done"][slot] = st3["mm1_pe_last"]
            c_silu_marker = p.marker("act")
            if late_hook is not None and e_ == NE - 1:
                late_hook()

            if e_ == 0:
                emit_bias_init()
                c_re_dve = emit_Re(0)
                if half == 0 and not st3["casts_emitted"]:
                    st3["casts_emitted"] = True
                    st3["c_wq_cast"] = p.inc("pool", lambda e: e.dma_start(out=wq_bf.ap(), in_=d_qkvw), "swq", 16)
                    st3["c_wp_cast"] = p.inc("pool", lambda e: e.dma_start(out=wp_bf.ap(), in_=d_projw), "swp", 16)

            for mp in range(4):
                wslot = (gi * 4 + mp) % 2
                sem_w2 = "sw2%d" % wslot
                if not (gi == 0 and mp < 1):
                    if st3["w2_uses"][wslot] >= 1:
                        p.wait("pool", "pe", st3["w2_pe_done"][wslot])
                    if half == 1 and e_ == 0 and wslot == 1:
                        p.wait("pool", "pe", st3["k_readers_pe"])
                    p.inc("pool", lambda e, e_=e_, mp=mp, wslot=wslot: e.dma_start(
                        out=w2bufs[wslot][:, :, :],
                        in_=d_w2[e_, :, mp * 256:(mp + 1) * 256].rearrange("(c p) d -> p c d", p=128)),
                        sem_w2, 16)
                    st3["w2_uses"][wslot] += 1
                p.wait("pe", sem_w2, 16 * st3["w2_uses"][wslot])
                if mp == 0:
                    p.wait("pe", "act", c_silu_marker)
                for m_in in range(2):
                    m = mp * 2 + m_in
                    if mp == 3 and m_in == 1:
                        st3["pse_uses"] += 1
                        if st3["pse_uses"] >= 2:
                            p.wait("pe", "pse", 2 * (st3["pse_uses"] - 1))
                        tgt = [(psE[0], "pse"), (psE[1], "pse")]
                    else:
                        sls = [quad.acquire(), quad.acquire()]
                        tgt = [(ps_q[:, sls[n][0], :], sls[n][1]) for n in range(2)]
                    for hc in range(32):
                        for n in range(2):
                            fn = lambda e, t_=tgt[n][0], hc=hc, n=n, m_in=m_in, wslot=wslot: e.matmul(
                                t_, w2bufs[wslot][:, hc, m_in * 128:(m_in + 1) * 128],
                                sh[:, hc, n * 512:(n + 1) * 512],
                                start=(hc == 0), stop=(hc == 31))
                            if hc == 31 and n == 1:
                                cpe = p.inc("pe", fn, "pe", 1)
                            else:
                                p.do("pe", fn)
                    st3["mm2_gate"] = cpe
                    p.wait("dve", "pe", cpe)
                    p.wait("dve", "act", c_re_dve)
                    for n in range(2):
                        p.inc("dve", lambda e, t_=tgt[n][0], n=n: e.tensor_tensor(
                            ytmp[:, n, :], t_, R_e[:, n * 512:(n + 1) * 512],
                            ALU.mult), tgt[n][1], 1)
                    p.dr()
                    for n in range(2):
                        p.inc("dve", lambda e, m=m, n=n: e.tensor_tensor(
                            y_acc[:, m, n * 512:(n + 1) * 512],
                            y_acc[:, m, n * 512:(n + 1) * 512], ytmp[:, n, :], ALU.add), "dve", 1)
                        p.dr()
                st3["w2_pe_done"][wslot] = st3["mm2_gate"]

        if half == 1:
            p.wait("sp", "pe", st3["w2_pe_done"][0])
            p.wait("sp", "swp", st3["c_wp_cast"])
            for dc in range(DC):
                st3["c_pw"] = p.inc("sp", lambda e, dc=dc: e.dma_start(
                    out=pwT[:, dc, :], in_=wp_bf.ap()[:, dc * 128:(dc + 1) * 128],
                    transpose=True), "spw2", 16)

        c_yacc = p.marker("dve")
        st3["c_yacc"] = c_yacc
        p.wait("act", "dve", c_yacc)
        st3["g_silu"][half] = p.inc("act", lambda e: e.activation(
            y_acc[:, :, :], y_acc[:, :, :], AF.Silu), "act", 1)
        p.wait("sp", "act", st3["g_silu"][half])
        st3["g_store"][half] = p.inc("sp", lambda e, half=half: e.dma_start(
            out=g_dram.ap()[:, half * W:(half + 1) * W].rearrange("(c p) t -> p c t", p=128),
            in_=y_acc[:, :, :]), "sg", 16)

    emit_p3_half(0)

    p3h0_pe = p.c("pe")
    p.wait("sp", "swq", st3["c_wq_cast"])

    def load_wT(part, buf, pe_gate, semname):
        p.wait("sp", "pe", pe_gate)
        cs = 0
        for dc in range(DC):
            cs = p.inc("sp", lambda e, dc=dc, part=part, buf=buf: e.dma_start(
                out=buf[:, dc, :],
                in_=wq_bf.ap()[part * 1024:(part + 1) * 1024, dc * 128:(dc + 1) * 128],
                transpose=True), semname, 16)
        return (semname, cs)


    p.wait("dve", "act", st3["g_silu"][0])
    p.wait("dve", "sg", st3["g_store"][0])
    p.wait("dve", "sw20", p.c("sw20"))
    p.wait("dve", "sw21", p.c("sw21"))
    p.wait("dve", "sxh", 16 * st3["xh_loads"])

    qk_pe_last = {0: 0, 1: 0}
    wT_of = {0: wT, 1: wT2, 2: wT}
    c_wT_next = {0: load_wT(0, wT, max(st3["w1_pe_done"]), "swta")}
    p.wait("sp", "shsp", c_hsp)
    p.wait("sp", "pe", p3h0_pe)
    for ch in range(4):
        p.inc("sp", lambda e, ch=ch: e.dma_start(
            out=h2[:, :, ch * 512:(ch + 1) * 512],
            in_=h_dram.ap()[:, ch * 512:(ch + 1) * 512].rearrange("(c p) t -> p c t", p=128)),
            "sh2c%d" % ch, 16)
    p.wait("sp", "act", st3["g_silu"][0])
    p.wait("sp", "sg", st3["g_store"][0])
    c_wT_next[1] = load_wT(1, wT2, p3h0_pe, "swtb")
    for part, dst in ((0, q_tok), (1, k_tok)):
        c_wT = c_wT_next[part]
        for js in range(2):
            for tc in range(NT):
                sl, sn = quad.acquire()
                p.wait("pe", c_wT[0], c_wT[1])
                if part == 0 and tc % 4 == 0:
                    p.wait("pe", "sh2c%d" % (tc // 4), 16)
                wTb = wT_of[part]
                for dc in range(DC):
                    fn = lambda e, sl=sl, dc=dc, tc=tc, js=js, wTb=wTb: e.matmul(
                        ps_q[:, sl, :], h2[:, dc, tc * 128:(tc + 1) * 128],
                        wTb[:, dc, js * 512:(js + 1) * 512],
                        start=(dc == 0), stop=(dc == DC - 1))
                    if dc < DC - 1:
                        p.do("pe", fn)
                    else:
                        cpe = p.inc("pe", fn, "pe", 1)
                qk_pe_last[part] = cpe
                p.wait("dve", "pe", cpe)
                p.inc("dve", lambda e, sl=sl, tc=tc, js=js, dst=dst, part=part: e.scalar_tensor_tensor(
                    dst[:, tc, js * 512:(js + 1) * 512], ps_q[:, sl, :], 1.0,
                    rep4[:, part * 2 + js, :], ALU.mult, ALU.add), sn, 1)

    qk_marker = p.marker("dve")
    c_wT = load_wT(2, wT, qk_pe_last[0], "swta")
    for vjc in range(DC):
        for ts in range(4):
            sl, sn = quad.acquire()
            p.wait("pe", c_wT[0], c_wT[1])
            for dc in range(DC):
                fn = lambda e, sl=sl, dc=dc, vjc=vjc, ts=ts: e.matmul(
                    ps_q[:, sl, :], wT_of[2][:, dc, vjc * 128:(vjc + 1) * 128],
                    h2[:, dc, ts * 512:(ts + 1) * 512],
                    start=(dc == 0), stop=(dc == DC - 1))
                if dc < DC - 1:
                    p.do("pe", fn)
                else:
                    cpe = p.inc("pe", fn, "pe", 1)
            k_ = vjc * 4 + ts
            eslot = k_ % 2
            p.wait("dve", "pe", cpe)
            if k_ >= 2:
                p.wait("dve", "svd%d" % eslot, 16 * (k_ // 2))
            p.inc("dve", lambda e, sl=sl, vjc=vjc, eslot=eslot: e.tensor_scalar(
                vstage[:, eslot, :], ps_q[:, sl, :],
                qkvb_v[:, vjc:vjc + 1], None, ALU.add), sn, 1)
            p.dr()
            cv2 = p.marker("dve")
            p.wait("sp", "dve", cv2)
            p.inc("sp", lambda e, vjc=vjc, ts=ts, eslot=eslot: e.dma_start(
                out=v_dram.ap()[vjc * 128:(vjc + 1) * 128, ts * 512:(ts + 1) * 512],
                in_=vstage[:, eslot, :]), "svd%d" % eslot, 16)

    st3["v_pe_last"] = cpe
    p.wait("pool", "pe", st3["v_pe_last"])
    for j in range(2):
        p.inc("pool", lambda e, j=j: e.dma_start(
            out=pfA[:, j, :, :],
            in_=d_w1[0, :, (2 + j) * 512:(3 + j) * 512].rearrange("(c p) h -> p c h", p=128)),
            "spfa", 16)
    for j in range(2):
        p.inc("pool", lambda e, j=j: e.dma_start(
            out=pfC[:, j, :, :],
            in_=d_w1[0, :, (4 + j) * 512:(5 + j) * 512].rearrange("(c p) h -> p c h", p=128)),
            "spfc", 16)
    p.wait("pool", "sg", st3["g_store"][0])
    p.wait("pool", "dve", st3["c_yacc"])
    for j in range(2):
        p.inc("pool", lambda e, j=j: e.dma_start(
            out=pfB[:, j, :, :],
            in_=d_w1[0, :, (6 + j) * 512:(7 + j) * 512].rearrange("(c p) h -> p c h", p=128)),
            "spfb", 16)
    p.wait("pe", "dve", qk_marker)
    s_store = 0
    scores_last = 0
    for qfc in range(DC):
        for ks in range(2):
            sl, sn = quad.acquire()
            for tc in range(NT):
                fn = lambda e, sl=sl, tc=tc, qfc=qfc, ks=ks: e.matmul(
                    ps_q[:, sl, :], q_tok[:, tc, qfc * 128:(qfc + 1) * 128],
                    k_tok[:, tc, ks * 512:(ks + 1) * 512],
                    start=(tc == 0), stop=(tc == NT - 1))
                if tc < NT - 1:
                    p.do("pe", fn)
                else:
                    cpe = p.inc("pe", fn, "pe", 1)
            scores_last = cpe
            p.wait("act", "pe", cpe)
            if qfc >= 1:
                p.wait("act", "sst", s_store)
            p.inc("act", lambda e, sl=sl, ks=ks: e.activation(
                S_stage[:, ks * 512:(ks + 1) * 512], ps_q[:, sl, :], AF.Copy,
                scale=SCALE_S), sn, 1)
        c_sev = p.marker("act")
        p.wait("sp", "act", c_sev)
        s_store = p.inc("sp", lambda e, qfc=qfc: e.dma_start(
            out=S_dram.ap()[qfc * 128:(qfc + 1) * 128, :], in_=S_stage[:, :]), "sst", 16)
    st3["k_readers_pe"] = scores_last

    p.wait("pool", "sst", s_store)
    p.inc("pool", lambda e: e.collective_compute(
        "AllReduce", ALU.add, replica_groups=groups,
        ins=[S_dram.ap()], outs=[Ssum_dram.ap()]), "cc", 1)

    if "q" in tap_aps:
        p.wait("pool", "dve", qk_marker)
        p.inc("pool", lambda e: e.dma_start(out=tap_aps["q"], in_=q_tok[:, :, :]), "stapp", 16)
    if "k" in tap_aps:
        p.wait("pool", "dve", qk_marker)
        p.inc("pool", lambda e: e.dma_start(out=tap_aps["k"], in_=k_tok[:, :, :]), "stapp", 16)
    if "v" in tap_aps:
        p.wait("pool", "svd0", p.c("svd0"))
        p.wait("pool", "svd1", p.c("svd1"))
        p.inc("pool", lambda e: e.dma_start(out=tap_aps["v"], in_=v_dram.ap()), "stapp", 16)
    if "Spart" in tap_aps:
        p.wait("pool", "sst", s_store)
        p.inc("pool", lambda e: e.dma_start(out=tap_aps["Spart"], in_=S_dram.ap()), "stapp", 16)
    if "Ssum" in tap_aps:
        p.wait("pool", "cc", 1)
        p.inc("pool", lambda e: e.dma_start(out=tap_aps["Ssum"], in_=Ssum_dram.ap()), "stapp", 16)

    if phase_limit < 4:
        return finish(nc, p, tap_aps)

    emit_p3_half(1, hook_after={
        1: lambda: emit_softmax_qfc([0, 1]),
        2: lambda: emit_softmax_qfc([2, 3]),
        3: lambda: emit_softmax_qfc([4, 5]),
        4: lambda: emit_softmax_qfc([6, 7]),
    }, late_hook=emit_watt_loads)

    if "g" in tap_aps:
        p.wait("sp", "sg", p.c("sg"))
        p.inc("sp", lambda e: e.dma_start(out=tap_aps["g"], in_=g_dram.ap()), "stap", 16)

    if phase_limit < 5:
        return finish(nc, p, tap_aps)

    p3_pe_end = p.c("pe")
    p3_dve_end = p.c("dve")
    p.wait("sp", "pe", p3_pe_end)
    p.wait("sp", "cc", 1)

    c_vl = {}
    c_vl[0] = p.inc("sp", lambda e: e.dma_start(
        out=v2[:, :, 0:512],
        in_=v_dram.ap()[:, 0:512].rearrange("(c p) t -> p c t", p=128)), "svl0", 16)
    c_pw = st3["c_pw"]
    p.inc("sp", lambda e: e.dma_start(
        out=x_stage[:, :, :],
        in_=dx[0:512, :].rearrange("(c p) d -> p c d", p=128)), "sxr", 16)
    p.wait("sp", "dve", st3["c_yacc"])
    for cs in range(1, 4):
        c_vl[cs] = p.inc("sp", lambda e, cs=cs: e.dma_start(
            out=v2[:, :, cs * 512:(cs + 1) * 512],
            in_=v_dram.ap()[:, cs * 512:(cs + 1) * 512].rearrange("(c p) t -> p c t", p=128)),
            "svl%d" % cs, 16)

    o_store = {}
    sg_done = {}
    a_done_cs = {}
    c_trs_prev = 0
    for cs in range(4):
        for qfc in range(DC):
            if qfc == 0:
                p.wait("pe", "svl%d" % cs, c_vl[cs])
                if cs == 0:
                    p.wait("pe", "swat", st3["c_wat"])
            sl, sn = quad.acquire()
            for kfc in range(DC):
                fn = lambda e, sl=sl, kfc=kfc, qfc=qfc, cs=cs: e.matmul(
                    ps_q[:, sl, :], w_attT[:, kfc, qfc * 128:(qfc + 1) * 128],
                    v2[:, kfc, cs * 512:(cs + 1) * 512],
                    start=(kfc == 0), stop=(kfc == DC - 1))
                if kfc < DC - 1:
                    p.do("pe", fn)
                else:
                    cpe = p.inc("pe", fn, "pe", 1)
            p.wait("act", "pe", cpe)
            p.inc("act", lambda e, sl=sl, qfc=qfc, cs=cs: e.activation(
                a_fm[:, qfc, cs * 512:(cs + 1) * 512], ps_q[:, sl, :], AF.Copy), sn, 1)
        a_done_cs[cs] = p.marker("act")

        if cs >= 1:
            p.wait("sp", "sout", o_store[cs - 1])
            p.wait("sp", "dve", p.c("dve"))
            p.inc("sp", lambda e, cs=cs: e.dma_start(
                out=x_stage[:, :, :],
                in_=dx[cs * 512:(cs + 1) * 512, :].rearrange("(c p) d -> p c d", p=128)), "sxr", 16)
        def emit_tr(pend):
            m_, sgslot, cd_sg_ = pend
            s32, n32 = t32m.acquire()
            p.wait("pe", "pool", cd_sg_)
            for j in range(4):
                fn = lambda e, j=j, sgslot=sgslot: e.transpose(
                    ps_t32[:, j, :], sgbuf[:, sgslot, j * 128:(j + 1) * 128], ident_f[:, :])
                if j < 3:
                    p.do("pe", fn)
                else:
                    c_trs = p.inc("pe", fn, "pe", 1)
            p.wait("dve", "pe", c_trs)
            p.wait("dve", "sxr", 16 * (cs + 1))
            for j in range(4):
                last = (j == 3)
                p.inc("dve", lambda e, j=j, m_=m_: e.tensor_tensor(
                    o_stage[:, j, m_ * 128:(m_ + 1) * 128], ps_t32[:, j, :],
                    x_stage[:, j, m_ * 128:(m_ + 1) * 128], ALU.add), n32 if last else "dve", 1)
            p.dr()
            return c_trs

        pend = None
        for m in range(DC):
            sl, sn = quad.acquire()
            p.wait("pe", "act", a_done_cs[cs])
            if cs == 0 and m == 0:
                p.wait("pe", "spw2", c_pw)
            for qc in range(DC):
                fn = lambda e, sl=sl, qc=qc, m=m, cs=cs: e.matmul(
                    ps_q[:, sl, :], pwT[:, qc, m * 128:(m + 1) * 128],
                    a_fm[:, qc, cs * 512:(cs + 1) * 512],
                    start=(qc == 0), stop=(qc == DC - 1))
                if qc < DC - 1:
                    p.do("pe", fn)
                else:
                    cpe = p.inc("pe", fn, "pe", 1)
            k_ep = cs * DC + m
            eslot = k_ep % 2
            sgslot = k_ep % 2
            if k_ep >= 2:
                p.wait("sp", "pool", sg_done[k_ep - 2])
            p.inc("sp", lambda e, cs=cs, m=m, eslot=eslot: e.dma_start(
                out=xn_t[:, eslot, :], in_=xn_dram.ap()[m * 128:(m + 1) * 128, cs * 512:(cs + 1) * 512]),
                "sxnl%d" % eslot, 16)
            if cs == 0 and m == 0:
                p.wait("sp", "sg", st3["g_store"][0])
            if cs == 2 and m == 0:
                p.wait("sp", "sg", p.c("sg"))
            p.inc("sp", lambda e, cs=cs, m=m, eslot=eslot: e.dma_start(
                out=g_t[:, eslot, :], in_=g_dram.ap()[m * 128:(m + 1) * 128, cs * 512:(cs + 1) * 512]),
                "sgl%d" % eslot, 16)
            p.wait("dve", "pe", cpe)
            if k_ep >= 1:
                p.wait("dve", "pool", sg_done.get(k_ep - 1, 0))
            cd_pt = p.inc("dve", lambda e, sl=sl, m=m: e.tensor_scalar(
                ptmp[:, :], ps_q[:, sl, :], projb_pp[:, m:m + 1], None, ALU.add), sn, 1)
            c_ptm = p.marker("dve")
            p.wait("pool", "dve", c_ptm)
            p.wait("pool", "sxnl%d" % eslot, 16 * ((k_ep // 2) + 1))
            p.wait("pool", "sgl%d" % eslot, 16 * ((k_ep // 2) + 1))
            p.inc("pool", lambda e, eslot=eslot: e.tensor_tensor(
                ptmp[:, :], ptmp[:, :], xn_t[:, eslot, :], ALU.add), "pool", 1)
            p.wait("pool", "pool", p.c("pool"))
            if k_ep >= 2:
                p.wait("pool", "pe", c_trs_prev)
            cd_sg = p.inc("pool", lambda e, eslot=eslot, sgslot=sgslot: e.tensor_tensor(
                sgbuf[:, sgslot, :], ptmp[:, :], g_t[:, eslot, :], ALU.mult), "pool", 1)
            sg_done[k_ep] = cd_sg
            if pend is not None:
                c_trs_prev = emit_tr(pend)
            pend = (m, sgslot, cd_sg)
        c_trs_prev = emit_tr(pend)
        pend = None
        c_om = p.marker("dve")
        p.wait("sp", "dve", c_om)
        o_store[cs] = p.inc("sp", lambda e, cs=cs: e.dma_start(
            out=d_out[cs * 512:(cs + 1) * 512, :].rearrange("(c p) d -> p c d", p=128),
            in_=o_stage[:, :, :]), "sout", 16)

    return finish(nc, p, tap_aps)


def finish(nc, p, tap_aps):
    for sname in list(p.cnt):
        if p.cnt[sname] > 0 and sname != "sp":
            p.wait("sp", sname, p.cnt[sname])
    with nc.Block() as block:
        @block.tensor
        def _(e):
            for f in p.ops["pe"]:
                f(e)

        @block.scalar
        def _(e):
            for f in p.ops["act"]:
                f(e)

        @block.vector
        def _(e):
            for f in p.ops["dve"]:
                f(e)

        @block.gpsimd
        def _(e):
            for f in p.ops["pool"]:
                f(e)

        @block.sync
        def _(e):
            for f in p.ops["sp"]:
                f(e)
    return nc


def kernel(L0_attn_x, L0_attn_norm_w, L0_attn_norm_b, L0_attn_qkv_w, L0_attn_qkv_b,
           L0_attn_proj_w, L0_attn_proj_b, L0_ffn_gate_w, L0_ffn_gate_b,
           L0_ffn_w1, L0_ffn_b1, L0_ffn_w2, L0_ffn_b2):
    from concourse.bass_utils import run_bass_kernel_spmd
    x = np.ascontiguousarray(np.asarray(L0_attn_x, dtype=np.float32))
    weights = {
        "norm_w": np.ascontiguousarray(np.asarray(L0_attn_norm_w, np.float32)),
        "norm_b": np.ascontiguousarray(np.asarray(L0_attn_norm_b, np.float32)),
        "qkv_w": np.ascontiguousarray(np.asarray(L0_attn_qkv_w, np.float32)),
        "qkv_b": np.ascontiguousarray(np.asarray(L0_attn_qkv_b, np.float32)),
        "proj_w": np.ascontiguousarray(np.asarray(L0_attn_proj_w, np.float32)),
        "proj_b": np.ascontiguousarray(np.asarray(L0_attn_proj_b, np.float32)),
        "gate_w": np.ascontiguousarray(np.asarray(L0_ffn_gate_w, np.float32)),
        "gate_b": np.ascontiguousarray(np.asarray(L0_ffn_gate_b, np.float32)),
        "w1": np.ascontiguousarray(np.asarray(L0_ffn_w1, np.float32)),
        "b1": np.ascontiguousarray(np.asarray(L0_ffn_b1, np.float32)),
        "w2": np.ascontiguousarray(np.asarray(L0_ffn_w2, np.float32)),
        "b2": np.ascontiguousarray(np.asarray(L0_ffn_b2, np.float32)),
    }
    nc = build_program(groups=[[0, 1, 2, 3], [4, 5, 6, 7]])
    in_maps = []
    for c in range(8):
        n, part = c // 4, c % 4
        m = {"x": np.ascontiguousarray(x[n, part * T:(part + 1) * T, :])}
        m.update(weights)
        in_maps.append(m)
    res = run_bass_kernel_spmd(nc, in_maps, core_ids=list(range(8)))
    out = np.empty((2, 8192, 1024), dtype=np.float32)
    for c in range(8):
        n, part = c // 4, c % 4
        out[n, part * T:(part + 1) * T, :] = res.results[c]["out"]
    return out
